# version 14
# baseline (speedup 1.0000x reference)
import contextlib
import numpy as np
import ml_dtypes
import concourse.bass as bass
import concourse.mybir as mybir
from concourse.bass_utils import run_bass_kernel_spmd

F32 = mybir.dt.float32
BF16 = mybir.dt.bfloat16
I32 = mybir.dt.int32
AF = mybir.ActivationFunctionType
ALU = mybir.AluOpType
AX = mybir.AxisListType
BF = ml_dtypes.bfloat16

B, S, D = 2, 4096, 2048
E = 8
TOKS = 1024
NCH, TC = 8, 512
EXPB = -20.0
ESC = 0.125
LN_EPS = 1e-5
INV_D = 1.0 / 2048.0
ENGINES = ("sync", "gpsimd", "vector", "scalar", "tensor")


class Em:
    def __init__(self, sems):
        self.sems = sems
        self.ops = {e: [] for e in ENGINES}
        self.cnt = {k: 0 for k in sems}

    def do(self, eng, fn, inc=None, by=1):
        if inc is None:
            self.ops[eng].append(fn)
            return None
        self.cnt[inc] += by
        tgt = self.cnt[inc]
        sem = self.sems[inc]

        def g(h, fn=fn, sem=sem, by=by):
            fn(h).then_inc(sem, by)

        self.ops[eng].append(g)
        return tgt

    def wait(self, eng, inc, val):
        if val is None or val <= 0:
            return
        sem = self.sems[inc]
        self.ops[eng].append(lambda h, sem=sem, val=val: h.wait_ge(sem, val))

    def barrier(self):
        self.cnt["bar"] += len(ENGINES)
        tgt = self.cnt["bar"]
        sem = self.sems["bar"]
        for e in ENGINES:
            self.ops[e].append(lambda h, sem=sem: h.sem_inc(sem, 1))
            self.ops[e].append(lambda h, sem=sem, tgt=tgt: h.wait_ge(sem, tgt))


def DMA(out, in_, **kw):
    return lambda h, out=out, in_=in_, kw=kw: h.dma_start(out=out, in_=in_, **kw)


def MM(out, l, r, st=True, sp=True):
    return lambda h, out=out, l=l, r=r, st=st, sp=sp: h.matmul(
        out, l, r, start=st, stop=sp
    )


def TT(out, a, b, op):
    return lambda h, out=out, a=a, b=b, op=op: h.tensor_tensor(out, a, b, op)


def CP(out, a):
    return lambda h, out=out, a=a: h.tensor_copy(out, a)


def build(dbg=False, upto=5):
    nc = bass.Bass("TRN2", target_bir_lowering=False, debug=False)

    t_xb = nc.dram_tensor("xb", [S, D], F32, kind="ExternalInput").ap()
    t_xown = nc.dram_tensor("xown", [TOKS, D], F32, kind="ExternalInput").ap()
    t_wqk = nc.dram_tensor("wqk", [128, 16 * 8 * 128], F32, kind="ExternalInput").ap()
    t_bqk = nc.dram_tensor("bqk", [128, 8], F32, kind="ExternalInput").ap()
    t_wv = nc.dram_tensor("wv", [128, 16 * 520], F32, kind="ExternalInput").ap()
    t_bv = nc.dram_tensor("bv", [1, 520], F32, kind="ExternalInput").ap()
    t_csc = nc.dram_tensor("csc", [128, S], F32, kind="ExternalInput").ap()
    t_css = nc.dram_tensor("css", [128, S], F32, kind="ExternalInput").ap()
    t_sel = nc.dram_tensor("sel", [128, 10 * 128], BF16, kind="ExternalInput").ap()
    t_msk = nc.dram_tensor("msk", [128, 4 * 512], BF16, kind="ExternalInput").ap()
    t_idn = nc.dram_tensor("idn", [128, 128], F32, kind="ExternalInput").ap()
    t_wout = nc.dram_tensor("wout", [4, 128, 16 * 512], F32, kind="ExternalInput").ap()
    t_bout = nc.dram_tensor("bout", [1, D], F32, kind="ExternalInput").ap()
    t_gate = nc.dram_tensor("gate", [128, 16 * 8], F32, kind="ExternalInput").ap()
    t_wi0 = nc.dram_tensor("wi0", [E, 4, 128, 16 * 512], F32, kind="ExternalInput").ap()
    t_wi1 = nc.dram_tensor("wi1", [E, 4, 128, 16 * 512], F32, kind="ExternalInput").ap()
    t_wo = nc.dram_tensor("wo", [E, 4, 128, 16 * 512], F32, kind="ExternalInput").ap()
    t_agi = nc.dram_tensor("agi", [128, 16], I32, kind="ExternalInput").ap()
    t_y = nc.dram_tensor("y", [TOKS, D], F32, kind="ExternalOutput").ap()

    cc_in_h = nc.dram_tensor("cc_in", [512, S], BF16)
    cc_out_h = nc.dram_tensor("cc_out", [4 * 512, S], BF16)

    dbg_t = {}
    if dbg:
        for nm, shape, dt in [
            ("d_lnxT", [128, 16 * 512], BF16),
            ("d_qT", [128, 4 * S], BF16),
            ("d_kT", [128, 4 * S], BF16),
            ("d_V", [128, 32 * 520], BF16),
            ("d_ctx", [128, 4 * S], BF16),
            ("d_ctxall", [128, 16 * 1024], BF16),
            ("d_h2T", [128, 16 * 1024], BF16),
            ("d_gate", [8, 1024], F32),
            ("d_cw", [128, 64], F32),
        ]:
            dbg_t[nm] = nc.dram_tensor(nm, shape, dt, kind="ExternalOutput").ap()

    xb_r = t_xb.rearrange("(n p) d -> n p d", p=128)
    xown_r = t_xown.rearrange("(n p) d -> n p d", p=128)
    y_r = t_y.rearrange("(n p) d -> n p d", p=128)
    y_q = t_y.rearrange("(n p) d -> p n d", p=128)

    es = contextlib.ExitStack()
    with es:
        sems = {
            n: es.enter_context(nc.semaphore(n))
            for n in ["lds", "ldp", "sts", "stp", "cc", "pe", "dve", "act", "bar",
                      "wv0", "wv1", "wv2", "wv3", "xa", "xb", "cs", "gat",
                      "woa", "wob", "wsa", "wsb", "wsc", "wq0", "wq1", "wq2", "wq3",
                      "wo0", "wo1", "wo2", "wo3", "wo4", "wo5", "wo6", "wo7",
                      "ya", "yb", "bn0", "bn1",
                      "p30", "p31", "p32", "p33", "p34", "p35", "p36", "p37"]
        }
        em = Em(sems)
        W = em.wait
        C = em.cnt

        A = nc.alloc_sbuf_tensor
        consts = A("consts", [128, 8], F32)
        ones_bf = A("ones_bf", [128, 128], BF16)
        ones_f = A("ones_f", [128, 128], F32)
        idn = A("idn_sb", [128, 128], F32)
        sel = A("sel_sb", [128, 10 * 128], BF16)
        msk = A("msk_sb", [128, 4 * 512], BF16)
        bqk = A("bqk_sb", [128, 8], F32)
        bv = A("bv_sb", [1, 520], BF16)
        bout = A("bout_sb", [1, D], BF16)
        gsb = A("gate_f32", [128, 128], F32)
        ghi = A("gate_hi", [128, 128], BF16)
        glo = A("gate_lo", [128, 128], BF16)
        cwsb = A("cw_sb", [128, 64], F32)
        agix = A("agi_sb", [128, 16], I32)
        wvt = A("wvt", [128, 4 * 520], BF16)

        B1 = A("B1", [128, 16384], BF16)
        B2 = A("B2", [128, 16384], BF16)
        B3 = A("B3", [128, 16384], BF16)
        B4 = A("B4", [128, 16640], BF16)
        B5 = A("B5", [128, 12288], BF16)

        F1 = A("F1", [128, 4096], F32)
        F2 = A("F2", [128, 2048], F32)
        F3 = A("F3", [128, 1024], F32)
        F4 = A("F4", [128, 1024], F32)
        F5 = A("F5", [128, 32], F32)

        PSA = nc.alloc_psum_tensor("PSA", [128, 1536], F32)
        PSB = nc.alloc_psum_tensor("PSB", [128, 1536], F32)
        PSC = nc.alloc_psum_tensor("PSC", [128, 512], F32)
        PSD = nc.alloc_psum_tensor("PSD", [128, 512], F32)

        CONSTS = [ESC, EXPB, INV_D, LN_EPS, -0.5, 1.0, 0.0, -INV_D]
        for i, v in enumerate(CONSTS):
            nc.const_aps.register(F32, v, consts[:, i : i + 1])

        block = es.enter_context(nc.Block())

        for i, v in enumerate(CONSTS):
            em.do("vector", lambda h, a=consts[:, i : i + 1], v=v: h.memset(a, v))
        em.do("vector", lambda h: h.memset(ones_bf[:], 1.0))
        em.do("vector", lambda h: h.memset(ones_f[:], 1.0), inc="dve")

        em.do("sync", DMA(idn[:], t_idn), inc="lds", by=16)
        em.do("sync", DMA(sel[:], t_sel), inc="lds", by=16)
        em.do("sync", DMA(msk[:], t_msk), inc="lds", by=16)
        em.do("sync", DMA(bqk[:], t_bqk), inc="lds", by=16)
        em.do("sync", DMA(agix[:], t_agi), inc="lds", by=16)
        em.do("sync", DMA(gsb[:], t_gate), inc="lds", by=16)
        ld_small = C["lds"]
        em.do("gpsimd", DMA(B1[:], t_wqk), inc="ldp", by=16)
        em.do("gpsimd", DMA(bv[:], t_bv), inc="ldp", by=16)
        em.do("gpsimd", DMA(bout[:], t_bout), inc="ldp", by=16)
        ld_wqk = C["ldp"]

        W("vector", "lds", ld_small)
        em.do("vector", CP(ghi[:], gsb[:]), inc="dve")
        W("vector", "dve", C["dve"])
        em.do("vector", TT(glo[:], gsb[:], ghi[:], ALU.subtract), inc="dve")

        em.barrier()

        def emit_ln(xs, junk):
            em.do("vector", lambda h, xs=xs: h.reduce_sum(F5[:, 0:1], xs, AX.X), inc="dve")
            W("vector", "dve", C["dve"])
            em.do(
                "vector",
                lambda h: h.tensor_scalar_mul(F5[:, 1:2], F5[:, 0:1], -INV_D),
                inc="dve",
            )
            W("scalar", "dve", C["dve"])
            em.do(
                "scalar",
                lambda h, xs=xs, junk=junk: h.activation(
                    junk, xs, AF.Square, bias=F5[:, 1:2], accum_out=F5[:, 2:3]
                ),
            )
            em.do(
                "scalar",
                lambda h: h.activation(
                    F5[:, 3:4], F5[:, 2:3], AF.Ln, bias=LN_EPS, scale=INV_D
                ),
            )
            em.do(
                "scalar",
                lambda h: h.activation(F5[:, 4:5], F5[:, 3:4], AF.Exp, scale=-0.5),
                inc="act",
            )
            W("vector", "act", C["act"])
            em.do(
                "vector",
                lambda h, xs=xs: h.tensor_scalar(
                    xs, xs, F5[:, 1:2], F5[:, 4:5], ALU.add, ALU.mult
                ),
                inc="dve",
            )

        W("tensor", "lds", ld_small)
        W("tensor", "ldp", ld_wqk)
        pe_lnxT_consumed = 0
        f1_guard = [0, 0]
        wvt_guard = [0, 0, 0, 0]
        tp_regs = [
            [PSA[:, 0:512], PSA[:, 512:1024], PSA[:, 1024:1536], PSC[:, 0:512]],
            [PSB[:, 0:512], PSB[:, 512:1024], PSB[:, 1024:1536], PSD[:, 0:512]],
        ]
        tp_region_guard = [[0] * 4, [0] * 4]
        B5v = B5[:, 0:8192].rearrange("p (dh f) -> p dh f", dh=16)
        dbg_done = {}

        for ch in range(NCH):
            W("sync", "dve", dbg_done.get("rope_prev", 0))
            em.do("sync", DMA(F4[:, 0:512], t_csc[:, ch * TC : (ch + 1) * TC]), inc="cs", by=16)
            em.do("sync", DMA(F4[:, 512:1024], t_css[:, ch * TC : (ch + 1) * TC]), inc="cs", by=16)
            cs_ld = C["cs"]
            wv_ld = [None] * 16
            for dh in range(4):
                W("gpsimd", "pe", wvt_guard[dh])
                em.do(
                    "gpsimd",
                    DMA(wvt[:, dh * 520 : (dh + 1) * 520], t_wv[:, dh * 520 : (dh + 1) * 520]),
                    inc=f"wv{dh}",
                    by=16,
                )
                wv_ld[dh] = (f"wv{dh}", C[f"wv{dh}"])

            for t4 in range(4):
                ti = ch * 4 + t4
                sl = ti % 2
                xs = F1[:, sl * 2048 : (sl + 1) * 2048]
                W("sync", "pe", f1_guard[sl])
                xsem = ["xa", "xb"][sl]
                em.do("sync", DMA(xs, xb_r[ti]), inc=xsem, by=16)
                W("vector", xsem, C[xsem])
                emit_ln(xs, F2[:, 0:2048])
                ln_done = C["dve"]
                W("tensor", "dve", ln_done)
                par4 = t4 % 2
                regs = tp_regs[par4]
                for g in range(4):
                    W("tensor", "dve", tp_region_guard[par4][g])
                    for q in range(4):
                        dh = g * 4 + q
                        ps = regs[g][:, q * 128 : (q + 1) * 128]
                        em.do(
                            "tensor",
                            lambda h, ps=ps, xs=xs, dh=dh: h.transpose(
                                ps, xs[:, dh * 128 : (dh + 1) * 128], idn[:]
                            ),
                            inc="pe" if q == 3 else None,
                        )
                    W("vector", "pe", C["pe"])
                    if t4 == 0 and g == 0:
                        W("vector", "pe", pe_lnxT_consumed)
                    dst = B5v[:, g * 4 : g * 4 + 4, t4 * 128 : t4 * 128 + 128]
                    src = regs[g].rearrange("p (a b) -> p a b", a=4)
                    tp_region_guard[par4][g] = em.do(
                        "vector", CP(dst, src), inc="dve"
                    )
                f1_guard[sl] = C["pe"]
            lnxT_ready = C["dve"]

            if dbg and ch == 0:
                W("sync", "dve", lnxT_ready)
                em.do("sync", DMA(dbg_t["d_lnxT"], B5[:, 0:8192]), inc="sts", by=16)

            W("tensor", "dve", lnxT_ready)
            evac_guard = [0, 0]
            qkv_pe = [0] * 8
            for ct in range(8):
                ps = [PSC, PSD][ct % 2]
                W("tensor", "dve", evac_guard[ct % 2])
                for dh in range(16):
                    em.do(
                        "tensor",
                        MM(
                            ps[:, 0:512],
                            B1[:, (dh * 8 + ct) * 128 : (dh * 8 + ct) * 128 + 128],
                            B5[:, dh * 512 : (dh + 1) * 512],
                            st=(dh == 0),
                            sp=(dh == 15),
                        ),
                        inc="pe" if dh == 15 else None,
                    )
                qkv_pe[ct] = C["pe"]
                W("vector", "pe", qkv_pe[ct])
                bcol = bqk[:, ct : ct + 1]
                if ct in (0, 1, 4, 5):
                    off = [0, 512, None, None, 1024, 1536][ct]
                    dst = F2[:, off : off + 512]
                    evac_guard[ct % 2] = em.do(
                        "vector",
                        lambda h, dst=dst, ps=ps, bcol=bcol: h.tensor_scalar(
                            dst, ps[:, 0:512], bcol, None, ALU.add
                        ),
                        inc="dve",
                    )
                else:
                    off = {2: 10240, 3: 10752, 6: 11264, 7: 11776}[ct]
                    dst = B5[:, off : off + 512]
                    evac_guard[ct % 2] = em.do(
                        "vector",
                        lambda h, dst=dst, ps=ps, bcol=bcol: h.tensor_scalar(
                            dst, ps[:, 0:512], bcol, None, ALU.add
                        ),
                        inc="dve",
                    )

            W("vector", "cs", cs_ld)
            CC, SS = F4[:, 0:512], F4[:, 512:1024]
            for base, rout in ((0, 8192), (1024, 9216)):
                u1 = F2[:, base : base + 512]
                u2 = F2[:, base + 512 : base + 1024]
                em.do("vector", TT(F3[:, 0:512], u1, CC, ALU.mult))
                em.do("vector", TT(F3[:, 512:1024], u2, SS, ALU.mult))
                em.do(
                    "vector",
                    TT(B5[:, rout : rout + 512], F3[:, 0:512], F3[:, 512:1024], ALU.subtract),
                )
                em.do("vector", TT(F3[:, 0:512], u1, SS, ALU.mult))
                em.do("vector", TT(F3[:, 512:1024], u2, CC, ALU.mult))
                em.do(
                    "vector",
                    TT(B5[:, rout + 512 : rout + 1024], F3[:, 0:512], F3[:, 512:1024], ALU.add),
                    inc="dve",
                )
            rope_done = C["dve"]
            dbg_done["rope_prev"] = rope_done

            W("tensor", "dve", rope_done)
            perm_evac = 0
            for isq, (rotoff, nroff, dstT) in enumerate(
                ((8192, 10240, B2), (9216, 11264, B3))
            ):
                for r in range(4):
                    W("tensor", "dve", perm_evac)
                    su1 = sel[:, r * 128 : (r + 1) * 128]
                    su2 = sel[:, (4 + r) * 128 : (5 + r) * 128]
                    snr = sel[:, (8 + r % 2) * 128 : (9 + r % 2) * 128]
                    nrsrc = B5[:, nroff + (r // 2) * 512 : nroff + (r // 2) * 512 + 512]
                    em.do("tensor", MM(PSA[:, 0:512], su1, B5[:, rotoff : rotoff + 512], st=True, sp=False))
                    em.do("tensor", MM(PSA[:, 0:512], su2, B5[:, rotoff + 512 : rotoff + 1024], st=False, sp=False))
                    em.do("tensor", MM(PSA[:, 0:512], snr, nrsrc, st=False, sp=True), inc="pe")
                    W("vector", "pe", C["pe"])
                    dst = dstT[:, r * S + ch * TC : r * S + ch * TC + 512]
                    perm_evac = em.do("vector", CP(dst, PSA[:, 0:512]), inc="dve")

            vmain = [PSA[:, 0:512], PSA[:, 512:1024], PSA[:, 1024:1536], PSB[:, 0:512]]
            vtail = [PSB[:, 512:520], PSB[:, 1024:1032], PSC[:, 128:136], PSD[:, 128:136]]
            W("tensor", "dve", perm_evac)
            for dh in range(16):
                slv = dh % 4
                W("tensor", wv_ld[dh][0], wv_ld[dh][1])
                for t4 in range(4):
                    lhs = B5[:, dh * 512 + t4 * 128 : dh * 512 + t4 * 128 + 128]
                    em.do(
                        "tensor",
                        MM(vmain[t4], lhs, wvt[:, slv * 520 : slv * 520 + 512], st=(dh == 0), sp=False),
                    )
                    em.do(
                        "tensor",
                        MM(vtail[t4], lhs, wvt[:, slv * 520 + 512 : slv * 520 + 520], st=(dh == 0), sp=False),
                        inc="pe" if t4 == 3 else None,
                    )
                wvt_guard[slv] = C["pe"]
                if dh + 4 < 16:
                    W("gpsimd", "pe", wvt_guard[slv])
                    em.do(
                        "gpsimd",
                        DMA(
                            wvt[:, slv * 520 : (slv + 1) * 520],
                            t_wv[:, (dh + 4) * 520 : (dh + 5) * 520],
                        ),
                        inc=f"wv{slv}",
                        by=16,
                    )
                    wv_ld[dh + 4] = (f"wv{slv}", C[f"wv{slv}"])
            for t4 in range(4):
                em.do("tensor", MM(vmain[t4], ones_bf[0:1, 0:128], bv[0:1, 0:512], st=False, sp=True))
                em.do(
                    "tensor",
                    MM(vtail[t4], ones_bf[0:1, 0:128], bv[0:1, 512:520], st=False, sp=True),
                    inc="pe",
                )
                W("vector", "pe", C["pe"])
                vc = (ch * 4 + t4) * 520
                em.do("vector", CP(B4[:, vc : vc + 512], vmain[t4]))
                em.do("vector", CP(B4[:, vc + 512 : vc + 520], vtail[t4]), inc="dve")
            pe_lnxT_consumed = C["pe"]
            tp_region_guard = [[C["dve"]] * 4, [C["dve"]] * 4]
            perm_evac = C["dve"]

        em.barrier()
        if dbg:
            em.do("sync", DMA(dbg_t["d_qT"], B2[:]), inc="sts", by=16)
            em.do("sync", DMA(dbg_t["d_kT"], B3[:]), inc="sts", by=16)
            em.do("sync", DMA(dbg_t["d_V"], B4[:, 0 : 32 * 520]), inc="sts", by=16)
            W("sync", "sts", C["sts"])
            em.barrier()

        if upto >= 1:
         sc_slots = [PSA[:, 0:1024], PSB[:, 0:1024]]
         bc_ps = [PSA[:, 1024:1536], PSB[:, 1024:1536]]
         av_ps = [PSC, PSD]
         slot_exp_act = [0, 0]
         attn_av_pe = [0, 0, 0]
         av_bank_dve = [C["dve"], C["dve"]]
         bc_free_dve = [C["dve"], C["dve"]]
         f3_free_dve = [C["dve"], C["dve"]]
         tmp_sts = [0, 0]
         ri = 0
         cc_q = cc_out_h.ap().rearrange("r (q t) -> (r q) t", q=4)
         for r in range(4):
             for j in range(8):
                 qof = j * TC
                 nkb = 4 * (j + 1)
                 G = nkb // 2
                 for s in range(2):
                     par = ri % 2
                     avb = av_ps[par]
                     hh = 2 * r + s
                     sc_pe = {}
                     exp_act = {}
                     msk_dve = {}

                     def emit_sc(g, s=s, r=r, qof=qof, sc_pe=sc_pe, exp_act=exp_act, msk_dve=msk_dve):
                         slot = sc_slots[g % 2]
                         W("tensor", "act", slot_exp_act[g % 2])
                         for i2 in range(2):
                             kb = 2 * g + i2
                             em.do(
                                 "tensor",
                                 MM(
                                     slot[:, i2 * 512 : i2 * 512 + 512],
                                     B3[64 * s : 64 * s + 64, r * S + kb * 128 : r * S + kb * 128 + 128],
                                     B2[64 * s : 64 * s + 64, r * S + qof : r * S + qof + 512],
                                 ),
                                 inc="pe" if i2 == 1 else None,
                             )
                         sc_pe[g] = C["pe"]
                         W("scalar", "pe", sc_pe[g])
                         W("scalar", "pe", attn_av_pe[g % 3])
                         aslot = B5[:, (g % 3) * 1024 : (g % 3) * 1024 + 1024]
                         em.do(
                             "scalar",
                             lambda h, aslot=aslot, slot=slot: h.activation(
                                 aslot, slot, AF.Exp, bias=EXPB, scale=ESC
                             ),
                             inc="act",
                         )
                         slot_exp_act[g % 2] = C["act"]
                         exp_act[g] = C["act"]
                         nd = 0
                         for i2 in range(2):
                             kb = 2 * g + i2
                             o = kb * 128 - qof
                             if o >= 0:
                                 W("vector", "act", exp_act[g])
                                 asl = B5[:, (g % 3) * 1024 + i2 * 512 : (g % 3) * 1024 + i2 * 512 + 512]
                                 oi = o // 128
                                 em.do(
                                     "vector",
                                     TT(asl, asl, msk[:, oi * 512 : oi * 512 + 512], ALU.mult),
                                     inc="dve",
                                 )
                                 nd += 1
                         msk_dve[g] = C["dve"] if nd else None

                     def emit_av(g, s=s, r=r, hh=hh, nkb=nkb, avb=avb, par=par, exp_act=exp_act, msk_dve=msk_dve):
                         W("tensor", "act", exp_act[g])
                         if msk_dve.get(g):
                             W("tensor", "dve", msk_dve[g])
                         if g == 0:
                             W("tensor", "dve", av_bank_dve[par])
                         for i2 in range(2):
                             kb = 2 * g + i2
                             em.do(
                                 "tensor",
                                 MM(
                                     avb[0:65, :],
                                     B4[:, kb * 520 + hh * 65 : kb * 520 + hh * 65 + 65],
                                     B5[:, (g % 3) * 1024 + i2 * 512 : (g % 3) * 1024 + i2 * 512 + 512],
                                     st=(kb == 0),
                                     sp=(kb == nkb - 1),
                                 ),
                                 inc="pe" if i2 == 1 else None,
                             )
                         attn_av_pe[g % 3] = C["pe"]

                     emit_sc(0)
                     if G > 1:
                         emit_sc(1)
                     for g in range(2, G):
                         emit_sc(g)
                         emit_av(g - 2)
                     if G > 1:
                         emit_av(G - 2)
                     emit_av(G - 1)
                     av_last = C["pe"]
                     W("vector", "pe", av_last)
                     frow = F4[64:65, par * 512 : par * 512 + 512]
                     em.do("vector", CP(frow, avb[64:65, :]), inc="dve")
                     W("tensor", "dve", C["dve"])
                     W("tensor", "dve", bc_free_dve[par])
                     em.do("tensor", MM(bc_ps[par], ones_f[64:65, 0:128], frow), inc="pe")
                     bc_pe = C["pe"]
                     W("vector", "pe", bc_pe)
                     W("vector", "dve", f3_free_dve[par])
                     f3s = F3[0:64, par * 512 : par * 512 + 512]
                     em.do(
                         "vector",
                         lambda h, f3s=f3s, b=bc_ps[par]: h.reciprocal(f3s, b[0:64, :]),
                         inc="dve",
                     )
                     bc_free_dve[par] = C["dve"]
                     if s == 0:
                         cdst = B1[0:64, r * S + qof : r * S + qof + 512]
                         em.do("vector", TT(cdst, avb[0:64, :], f3s, ALU.mult), inc="dve")
                     else:
                         W("vector", ["bn0", "bn1"][par], tmp_sts[par])
                         tmp = B5[0:64, 3072 + par * 512 : 3072 + par * 512 + 512]
                         em.do("vector", TT(tmp, avb[0:64, :], f3s, ALU.mult), inc="dve")
                         W("sync", "dve", C["dve"])
                         tmp_sts[par] = em.do(
                             "sync",
                             DMA(B1[64:128, r * S + qof : r * S + qof + 512], tmp),
                             inc=["bn0", "bn1"][par],
                             by=16,
                         )
                     av_bank_dve[par] = C["dve"]
                     f3_free_dve[par] = C["dve"]
                     ri += 1
             W("sync", "dve", C["dve"])
             W("sync", "bn0", C["bn0"])
             W("sync", "bn1", C["bn1"])
             em.do(
                 "sync",
                 DMA(cc_in_h.ap()[r * 128 : (r + 1) * 128, :], B1[:, r * S : (r + 1) * S]),
                 inc="sts",
                 by=16,
             )
             W("gpsimd", "sts", C["sts"])
             em.do(
                 "gpsimd",
                 lambda h, r=r: h.collective_compute(
                     "AllGather",
                     ALU.bypass,
                     replica_groups=[[0, 1, 2, 3], [4, 5, 6, 7]],
                     ins=[cc_in_h.ap()[r * 128 : (r + 1) * 128, :].opt()],
                     outs=[cc_out_h.ap()[r * 512 : (r + 1) * 512, :].opt()],
                 ),
                 inc="cc",
             )
             W("gpsimd", "cc", C["cc"])
             for rk in range(4):
                 c = r * 4 + rk
                 em.do(
                     "gpsimd",
                     lambda h, c=c: h.indirect_dma_start(
                         out=B1[:, c * 1024 : (c + 1) * 1024],
                         out_offset=None,
                         in_=cc_q[0:128, :],
                         in_offset=bass.IndirectOffsetOnAxis(ap=agix[:, c : c + 1], axis=0),
                     ),
                     inc="gat",
                     by=16,
                 )
             if r == 1:
                 W("gpsimd", "pe", C["pe"])
                 em.do("gpsimd", DMA(B3[:, 0:8192], t_wout[0]), inc="woa", by=16)
                 pre_wout = [("woa", C["woa"])]
             if r == 3:
                 W("gpsimd", "pe", C["pe"])
                 em.do("gpsimd", DMA(B3[:, 8192:16384], t_wout[1]), inc="wob", by=16)
                 pre_wout.append(("wob", C["wob"]))
         pre_x = []
         for tt in range(8):
             em.do("sync", DMA(F1[:, tt * 512 : tt * 512 + 512], xown_r[tt][:, 0:512]), inc=f"p3{tt}", by=16)
             pre_x.append((f"p3{tt}", C[f"p3{tt}"]))

         em.barrier()

        if upto >= 3:
         W("tensor", "gat", C["gat"])
         wout_guard = [C["pe"], C["pe"]]
         f1_g = [C["dve"]] * 8
         f2_g = [0, 0, 0, 0]
         ps4 = [PSA[:, 0:512], PSA[:, 512:1024], PSB[:, 0:512], PSB[:, 512:1024]]
         ps4_guard = [C["dve"]] * 4
         for dc in range(4):
             sl = dc % 2
             wsem = ["woa", "wob"][sl]
             if dc >= 2:
                 W("gpsimd", "pe", wout_guard[sl])
                 em.do(
                     "gpsimd",
                     DMA(B3[:, sl * 8192 : (sl + 1) * 8192], t_wout[dc]),
                     inc=wsem,
                     by=16,
                 )
                 wld = (wsem, C[wsem])
             else:
                 wld = pre_wout[dc]
             W("tensor", wld[0], wld[1])
             for tt in range(8):
                 it = dc * 8 + tt
                 fs = it % 8
                 if dc == 0:
                     xl = pre_x[tt]
                 else:
                     W("sync", "dve", f1_g[fs])
                     em.do(
                         "sync",
                         DMA(F1[:, fs * 512 : fs * 512 + 512], xown_r[tt][:, dc * 512 : dc * 512 + 512]),
                         inc=f"p3{fs}",
                         by=16,
                     )
                     xl = (f"p3{fs}", C[f"p3{fs}"])
                 ps = ps4[it % 4]
                 W("tensor", "dve", ps4_guard[it % 4])
                 for dh in range(16):
                     em.do(
                         "tensor",
                         MM(
                             ps,
                             B1[:, dh * 1024 + tt * 128 : dh * 1024 + tt * 128 + 128],
                             B3[:, sl * 8192 + dh * 512 : sl * 8192 + dh * 512 + 512],
                             st=(dh == 0),
                             sp=False,
                         ),
                     )
                 em.do(
                     "tensor",
                     MM(ps, ones_bf[0:1, 0:128], bout[0:1, dc * 512 : dc * 512 + 512], st=False, sp=True),
                     inc="pe",
                 )
                 mmd = C["pe"]
                 W("vector", "pe", mmd)
                 W("vector", xl[0], xl[1])
                 W("vector", "sts", f2_g[it % 4])
                 f2s = F2[:, (it % 4) * 512 : (it % 4) * 512 + 512]
                 em.do(
                     "vector",
                     TT(f2s, ps, F1[:, fs * 512 : fs * 512 + 512], ALU.add),
                     inc="dve",
                 )
                 ps4_guard[it % 4] = C["dve"]
                 f1_g[fs] = C["dve"]
                 W("sync", "dve", C["dve"])
                 f2_g[it % 4] = em.do(
                     "sync",
                     DMA(y_r[tt][:, dc * 512 : dc * 512 + 512], f2s),
                     inc="sts",
                     by=16,
                 )
             wout_guard[sl] = C["pe"]
             if dc == 2:
                 b3a_free_pe = C["pe"]
         em.do("gpsimd", DMA(B4[:, 0:8192], t_wi0[0, 0]), inc="wsc", by=16)
         pre_wi0_ld = ("wsc", C["wsc"])
         em.do("gpsimd", DMA(B4[:, 8192:16384], t_wo[0, 1]), inc="wo4", by=16)
         pre_wo1_ld = ("wo4", C["wo4"])
         W("gpsimd", "pe", b3a_free_pe)
         em.do("gpsimd", DMA(B3[:, 0:8192], t_wi1[0, 0]), inc="wq0", by=16)
         pre_wi1_ld = ("wq0", C["wq0"])
         W("sync", "sts", C["sts"])

         em.barrier()

        if upto >= 4:
         tpg = [C["dve"], C["dve"]]
         p4f1 = [C["pe"], C["pe"]]
         for tt in range(8):
             sl = tt % 2
             xs = F1[:, sl * 2048 : (sl + 1) * 2048]
             W("sync", "pe", p4f1[sl])
             ysem = ["ya", "yb"][sl]
             em.do("sync", DMA(xs, y_r[tt]), inc=ysem, by=16)
             W("vector", ysem, C[ysem])
             emit_ln(xs, F2[:, 0:2048])
             ln_done = C["dve"]
             W("tensor", "dve", ln_done)
             for dh in range(16):
                 ps = [PSC, PSD][dh % 2][:, 0:128]
                 W("tensor", "dve", tpg[dh % 2])
                 em.do(
                     "tensor",
                     lambda h, ps=ps, xs=xs, dh=dh: h.transpose(
                         ps, xs[:, dh * 128 : (dh + 1) * 128], idn[:]
                     ),
                     inc="pe",
                 )
                 W("scalar", "pe", C["pe"])
                 hidst = B1[:, dh * 1024 + tt * 128 : dh * 1024 + tt * 128 + 128]
                 em.do("scalar", lambda h, hidst=hidst, ps=ps: h.copy(hidst, ps), inc="act")
                 W("vector", "act", C["act"])
                 lodst = B5[:, dh * 512 + (tt % 4) * 128 : dh * 512 + (tt % 4) * 128 + 128]
                 tpg[dh % 2] = em.do(
                     "vector", TT(lodst, ps, hidst, ALU.subtract), inc="dve"
                 )
             p4f1[sl] = C["pe"]
             if tt % 4 == 3:
                 g4 = tt // 4
                 W("tensor", "dve", C["dve"])
                 for dh in range(16):
                     for term, (lw, rh) in enumerate(
                         (
                             (ghi, B1[:, dh * 1024 + g4 * 512 : dh * 1024 + g4 * 512 + 512]),
                             (glo, B1[:, dh * 1024 + g4 * 512 : dh * 1024 + g4 * 512 + 512]),
                             (ghi, B5[:, dh * 512 : dh * 512 + 512]),
                         )
                     ):
                         em.do(
                             "tensor",
                             MM(
                                 PSC[0:8, :],
                                 lw[:, dh * 8 : dh * 8 + 8],
                                 rh,
                                 st=(dh == 0 and term == 0),
                                 sp=(dh == 15 and term == 2),
                             ),
                             inc="pe" if (dh == 15 and term == 2) else None,
                         )
                 W("vector", "pe", C["pe"])
                 em.do(
                     "vector",
                     CP(F3[0:8, g4 * 512 : g4 * 512 + 512], PSC[0:8, :]),
                     inc="dve",
                 )
                 h2lo_guard = C["dve"]
                 gate_pe = C["pe"]
         W("gpsimd", "pe", gate_pe)
         em.do("gpsimd", DMA(B5[:, 0:8192], t_wo[0, 0]), inc="wo0", by=16)
         pre_wo0_ld = ("wo0", C["wo0"])
         if dbg:
             W("sync", "dve", C["dve"])
             em.do("sync", DMA(dbg_t["d_h2T"], B1[:]), inc="sts", by=16)
             em.do("sync", DMA(dbg_t["d_gate"], F3[0:8, :]), inc="sts", by=16)

         for tt in range(8):
             W("tensor", "dve", C["dve"])
             em.do(
                 "tensor",
                 lambda h, tt=tt: h.transpose(
                     PSD[0:128, 0:8],
                     F3[0:8, tt * 128 : tt * 128 + 128],
                     idn[0:8, 0:8],
                 ),
                 inc="pe",
             )
             W("vector", "pe", C["pe"])
             em.do("vector", CP(F5[:, 0:8], PSD[:, 0:8]), inc="dve")
             W("vector", "dve", C["dve"])
             em.do("vector", lambda h: h.reduce_max(F5[:, 8:9], F5[:, 0:8], AX.X), inc="dve")
             W("vector", "dve", C["dve"])
             em.do(
                 "vector",
                 lambda h: h.tensor_scalar(F5[:, 16:24], F5[:, 0:8], F5[:, 8:9], None, ALU.is_equal),
                 inc="dve",
             )
             W("vector", "dve", C["dve"])
             em.do(
                 "vector",
                 lambda h: h.scalar_tensor_tensor(
                     F5[:, 24:32], F5[:, 16:24], -1e30, F5[:, 0:8], ALU.mult, ALU.add
                 ),
                 inc="dve",
             )
             W("vector", "dve", C["dve"])
             em.do("vector", lambda h: h.reduce_max(F5[:, 9:10], F5[:, 24:32], AX.X), inc="dve")
             W("vector", "dve", C["dve"])
             em.do(
                 "vector",
                 lambda h: h.tensor_scalar(F5[:, 10:11], F5[:, 9:10], F5[:, 8:9], None, ALU.subtract),
                 inc="dve",
             )
             W("scalar", "dve", C["dve"])
             em.do("scalar", lambda h: h.activation(F5[:, 11:12], F5[:, 10:11], AF.Exp), inc="act")
             W("vector", "act", C["act"])
             em.do("vector", lambda h: h.tensor_scalar_add(F5[:, 12:13], F5[:, 11:12], 1.0), inc="dve")
             W("vector", "dve", C["dve"])
             em.do("vector", lambda h: h.reciprocal(F5[:, 13:14], F5[:, 12:13]), inc="dve")
             W("vector", "dve", C["dve"])
             em.do("vector", TT(F5[:, 14:15], F5[:, 11:12], F5[:, 13:14], ALU.mult), inc="dve")
             W("vector", "dve", C["dve"])
             em.do(
                 "vector",
                 lambda h: h.tensor_scalar(F5[:, 16:24], F5[:, 16:24], F5[:, 13:14], None, ALU.mult),
             )
             em.do(
                 "vector",
                 lambda h: h.tensor_scalar(F5[:, 24:32], F5[:, 24:32], F5[:, 9:10], None, ALU.is_equal),
                 inc="dve",
             )
             W("vector", "dve", C["dve"])
             em.do(
                 "vector",
                 lambda h, tt=tt: h.scalar_tensor_tensor(
                     cwsb[:, tt * 8 : tt * 8 + 8],
                     F5[:, 24:32],
                     F5[:, 14:15],
                     F5[:, 16:24],
                     ALU.mult,
                     ALU.add,
                 ),
                 inc="dve",
             )
         if dbg:
             W("sync", "dve", C["dve"])
             em.do("sync", DMA(dbg_t["d_cw"], cwsb[:]), inc="sts", by=16)

         em.barrier()

        if upto >= 5:
         wi_slots = [
             B3[:, 0:8192],
             B3[:, 8192:16384],
             B4[:, 0:8192],
         ]
         wi_guard = [C["pe"]] * 3
         wo_blks = [B5[:, 0:8192], B4[:, 8192:16384]]
         wo_q_ld = {0: [pre_wo0_ld] * 4, 1: [pre_wo1_ld] * 4}
         for _n in range(2, 4 * E):
             wo_q_ld[_n] = [None] * 4
         half_evac = [C["dve"], C["dve"]]
         seq = 2
         hps = [
             (PSA[:, 0:512], PSA[:, 512:1024]),
             (PSB[:, 0:512], PSB[:, 512:1024]),
         ]
         hps_guard = [C["dve"], C["dve"]]
         sil_guard = [C["dve"], C["dve"]]
         wops8 = [
             PSA[:, 0:512], PSA[:, 512:1024], PSA[:, 1024:1536],
             PSB[:, 0:512], PSB[:, 512:1024], PSB[:, 1024:1536],
             PSC[:, 0:512], PSD[:, 0:512],
         ]
         slot6 = [
             F2[:, 0:512], F2[:, 512:1024], F2[:, 1024:1536], F2[:, 1536:2048],
             F4[:, 0:512], F4[:, 512:1024],
         ]
         slot6_guard = [0] * 6
         y_wr_cnt = [0, 0, 0, 0]
         y_wr_new = [0, 0, 0, 0]
         wo_evac_dve = C["dve"]
         l1q = [pre_wi1_ld] * 4
         l0 = pre_wi0_ld
         wi_sems = ["wsa", "wsb", "wsc"]
         for e in range(E):
             for mc in range(4):
                 m = e * 4 + mc
                 s0 = seq % 3
                 s1 = (seq + 1) % 3
                 third = (seq + 2) % 3
                 seq += 2
                 if m + 1 < 4 * E:
                     en, mn = (m + 1) // 4, (m + 1) % 4
                     W("gpsimd", "pe", wi_guard[third])
                     em.do("gpsimd", DMA(wi_slots[third], t_wi0[en, mn]), inc=wi_sems[third], by=16)
                     l0_next = (wi_sems[third], C[wi_sems[third]])
                 l1q_next = [None] * 4
                 for ms in range(4):
                     for th in range(2):
                         it = ms * 2 + th
                         p01 = hps[it % 2]
                         W("tensor", "dve", hps_guard[it % 2])
                         if ms == 0 and th == 0:
                             W("tensor", l0[0], l0[1])
                         for dh in range(16):
                             em.do(
                                 "tensor",
                                 MM(
                                     p01[0],
                                     wi_slots[s0][:, ms * 2048 + dh * 128 : ms * 2048 + dh * 128 + 128],
                                     B1[:, dh * 1024 + th * 512 : dh * 1024 + th * 512 + 512],
                                     st=(dh == 0),
                                     sp=(dh == 15),
                                 ),
                                 inc="pe" if dh == 15 else None,
                             )
                         h0_pe = C["pe"]
                         if th == 0:
                             W("tensor", l1q[ms][0], l1q[ms][1])
                         for dh in range(16):
                             em.do(
                                 "tensor",
                                 MM(
                                     p01[1],
                                     wi_slots[s1][:, ms * 2048 + dh * 128 : ms * 2048 + dh * 128 + 128],
                                     B1[:, dh * 1024 + th * 512 : dh * 1024 + th * 512 + 512],
                                     st=(dh == 0),
                                     sp=(dh == 15),
                                 ),
                                 inc="pe" if dh == 15 else None,
                             )
                         h1_pe = C["pe"]
                         W("scalar", "pe", h0_pe)
                         W("scalar", "dve", sil_guard[it % 2])
                         fsl = F3[:, (it % 2) * 512 : (it % 2) * 512 + 512]
                         em.do(
                             "scalar",
                             lambda h, fsl=fsl, p=p01[0]: h.activation(fsl, p, AF.Silu),
                             inc="act",
                         )
                         sact = C["act"]
                         W("vector", "act", sact)
                         W("vector", "pe", h1_pe)
                         hdst = B2[:, (mc * 4 + ms) * 1024 + th * 512 : (mc * 4 + ms) * 1024 + th * 512 + 512]
                         em.do("vector", TT(hdst, fsl, p01[1], ALU.mult), inc="dve")
                         sil_guard[it % 2] = C["dve"]
                         hps_guard[it % 2] = C["dve"]
                     if m + 1 < 4 * E:
                         W("gpsimd", "pe", C["pe"])
                         em.do(
                             "gpsimd",
                             DMA(
                                 wi_slots[s0][:, ms * 2048 : (ms + 1) * 2048],
                                 t_wi1[en, mn][:, ms * 2048 : (ms + 1) * 2048],
                             ),
                             inc=f"wq{ms}",
                             by=16,
                         )
                         l1q_next[ms] = (f"wq{ms}", C[f"wq{ms}"])
                 wi_guard[s0] = C["pe"]
                 wi_guard[s1] = C["pe"]
                 if m + 1 < 4 * E:
                     l0, l1q = l0_next, l1q_next
             h_done_dve = C["dve"]

             W("sync", "dve", wo_evac_dve)
             W("sync", "stp", y_wr_cnt[0])
             em.do(
                 "sync",
                 DMA(
                     F1[:, 0:4096].rearrange("p (t c) -> p t c", t=8),
                     y_q[:, :, 0:512],
                 ),
                 inc="lds",
                 by=16,
             )
             y_rd = C["lds"]
             for dc in range(4):
                 n = e * 4 + dc
                 blk = wo_blks[n % 2]
                 if dc == 0:
                     W("tensor", "dve", h_done_dve)
                 for half in range(2):
                     W("tensor", "dve", half_evac[half])
                     for mh in range(16):
                         if mh % 4 == 0:
                             W("tensor", wo_q_ld[n][mh // 4][0], wo_q_ld[n][mh // 4][1])
                         for t4 in range(4):
                             tt = half * 4 + t4
                             em.do(
                                 "tensor",
                                 MM(
                                     wops8[tt],
                                     B2[:, mh * 1024 + tt * 128 : mh * 1024 + tt * 128 + 128],
                                     blk[:, mh * 512 : mh * 512 + 512],
                                     st=(mh == 0),
                                     sp=(mh == 15),
                                 ),
                                 inc="pe" if t4 == 3 else None,
                             )
                         if half == 1 and mh % 4 == 3 and n + 2 < 4 * E:
                             q = mh // 4
                             nb = wo_blks[(n + 2) % 2]
                             wsem2 = f"wo{((n + 2) % 2) * 4 + q}"
                             W("gpsimd", "pe", C["pe"])
                             em.do(
                                 "gpsimd",
                                 DMA(
                                     nb[:, q * 2048 : (q + 1) * 2048],
                                     t_wo[(n + 2) // 4, (n + 2) % 4][:, q * 2048 : (q + 1) * 2048],
                                 ),
                                 inc=wsem2,
                                 by=16,
                             )
                             wo_q_ld[n + 2][q] = (wsem2, C[wsem2])
                     half_pe = C["pe"]
                     W("vector", "pe", half_pe)
                     if half == 0:
                         W("vector", "lds", y_rd)
                     for t4 in range(4):
                         tt = half * 4 + t4
                         g6 = (dc * 8 + tt) % 6
                         W("vector", "stp", slot6_guard[g6])
                         sl6 = slot6[g6]
                         cwc = cwsb[:, tt * 8 + e : tt * 8 + e + 1]
                         em.do(
                             "vector",
                             lambda h, sl6=sl6, ps=wops8[tt], cwc=cwc, tt=tt: h.scalar_tensor_tensor(
                                 sl6, ps, cwc, F1[:, tt * 512 : tt * 512 + 512], ALU.mult, ALU.add
                             ),
                             inc="dve",
                         )
                         W("scalar", "dve", C["dve"])
                         slot6_guard[g6] = em.do(
                             "scalar",
                             DMA(y_r[tt][:, dc * 512 : dc * 512 + 512], sl6),
                             inc="stp",
                             by=16,
                         )
                     half_evac[half] = C["dve"]
                 wo_evac_dve = C["dve"]
                 y_wr_new[dc] = C["stp"]
                 if dc < 3:
                     W("sync", "dve", wo_evac_dve)
                     W("sync", "stp", y_wr_cnt[dc + 1])
                     em.do(
                         "sync",
                         DMA(
                             F1[:, 0:4096].rearrange("p (t c) -> p t c", t=8),
                             y_q[:, :, (dc + 1) * 512 : (dc + 2) * 512],
                         ),
                         inc="lds",
                         by=16,
                     )
                     y_rd = C["lds"]
             y_wr_cnt = list(y_wr_new)
             hps_guard = [wo_evac_dve, wo_evac_dve]

        W("gpsimd", "stp", C["stp"])
        W("sync", "sts", C["sts"])
        em.barrier()

        @block.sync
        def _(h):
            for f in em.ops["sync"]:
                f(h)

        @block.gpsimd
        def _(h):
            for f in em.ops["gpsimd"]:
                f(h)

        @block.vector
        def _(h):
            for f in em.ops["vector"]:
                f(h)

        @block.scalar
        def _(h):
            for f in em.ops["scalar"]:
                f(h)

        @block.tensor
        def _(h):
            for f in em.ops["tensor"]:
                f(h)

    return nc, dbg_t


def _stage_shared(Wqkv, b_qkv, Wout, b_out, G, wi0, wi1, wo):
    st = {}
    st["csc"], st["css"] = _rope_tables()
    st["sel"] = _selectors()
    st["msk"] = _masks()
    st["idn"] = np.eye(128, dtype=np.float32)
    cidx = np.arange(16)[:, None]
    perm = ((cidx % 4) * 512 + (cidx // 4) * 128 + np.arange(128)[None, :]).reshape(-1)
    st["wout"] = np.ascontiguousarray(
        Wout[perm].reshape(16, 128, 4, 512).transpose(2, 1, 0, 3).reshape(4, 128, 16 * 512)
    )
    st["bout"] = np.ascontiguousarray(b_out.reshape(1, D))
    st["gate"] = np.ascontiguousarray(G.reshape(16, 128, 8).transpose(1, 0, 2).reshape(128, 128))
    tr = lambda w: np.ascontiguousarray(
        w.reshape(E, 16, 128, 4, 512).transpose(0, 3, 2, 1, 4).reshape(E, 4, 128, 16 * 512)
    )
    tr_wi = lambda w: np.ascontiguousarray(
        w.reshape(E, 16, 128, 4, 4, 128).transpose(0, 3, 2, 4, 1, 5).reshape(E, 4, 128, 16 * 512)
    )
    st["wi0"], st["wi1"], st["wo"] = tr_wi(wi0), tr_wi(wi1), tr(wo)
    return st


def _rope_tables():
    invf = (10000.0 ** (-(np.arange(0, 32, 2, dtype=np.float64)) / 32.0)).astype(np.float64)
    t = np.arange(S, dtype=np.float64)
    f = np.outer(t, invf)
    c = np.cos(f).astype(np.float32).T
    s = np.sin(f).astype(np.float32).T
    return (
        np.ascontiguousarray(np.tile(c, (8, 1))),
        np.ascontiguousarray(np.tile(s, (8, 1))),
    )


def _selectors():
    sel = np.zeros((128, 10 * 128), dtype=BF)
    for r in range(4):
        u1 = np.zeros((128, 128), dtype=np.float32)
        u2 = np.zeros((128, 128), dtype=np.float32)
        for m in range(16):
            u1[32 * r + m, m] = 1.0
            u1[32 * r + 16 + m, 64 + m] = 1.0
            u2[32 * r + m, 16 + m] = 1.0
            u2[32 * r + 16 + m, 80 + m] = 1.0
        sel[:, r * 128 : (r + 1) * 128] = u1.astype(BF)
        sel[:, (4 + r) * 128 : (5 + r) * 128] = u2.astype(BF)
    nrA = np.zeros((128, 128), dtype=np.float32)
    nrB = np.zeros((128, 128), dtype=np.float32)
    for m in range(32):
        nrA[m, 32 + m] = 1.0
        nrA[32 + m, 96 + m] = 1.0
        nrB[64 + m, 32 + m] = 1.0
        nrB[96 + m, 96 + m] = 1.0
    sel[:, 8 * 128 : 9 * 128] = nrA.astype(BF)
    sel[:, 9 * 128 : 10 * 128] = nrB.astype(BF)
    return sel


def _masks():
    m = np.zeros((128, 4 * 512), dtype=BF)
    ki = np.arange(128)[:, None]
    qi = np.arange(512)[None, :]
    for oi, o in enumerate((0, 128, 256, 384)):
        m[:, oi * 512 : (oi + 1) * 512] = (qi >= ki + o).astype(BF)
    return m


def _stage_core(c, x, Wqkv, b_qkv):
    b, r = c // 4, c % 4
    heads = [8 * r + i for i in range(8)]
    d = {}
    d["xb"] = np.ascontiguousarray(x[b])
    d["xown"] = np.ascontiguousarray(x[b, r * TOKS : (r + 1) * TOKS])
    qu1 = [h * 64 + f for h in heads for f in range(16)]
    qu2 = [h * 64 + 16 + f for h in heads for f in range(16)]
    qnr = [h * 64 + 32 + dd for h in heads for dd in range(32)]
    cols = (
        qu1
        + qu2
        + qnr[:128]
        + qnr[128:]
        + [2048 + cc for cc in qu1]
        + [2048 + cc for cc in qu2]
        + [2048 + cc for cc in qnr[:128]]
        + [2048 + cc for cc in qnr[128:]]
    )
    Wqk = Wqkv[:, cols]
    d["wqk"] = np.ascontiguousarray(
        Wqk.reshape(16, 128, 8, 128).transpose(1, 0, 2, 3).reshape(128, 16 * 8 * 128)
    )
    d["bqk"] = np.ascontiguousarray(b_qkv[cols].reshape(8, 128).T)
    Wv = np.zeros((D, 520), dtype=np.float32)
    bve = np.zeros((1, 520), dtype=np.float32)
    for hh, h in enumerate(heads):
        Wv[:, hh * 65 : hh * 65 + 64] = Wqkv[:, 4096 + h * 64 : 4096 + h * 64 + 64]
        bve[0, hh * 65 : hh * 65 + 64] = b_qkv[4096 + h * 64 : 4096 + h * 64 + 64]
        bve[0, hh * 65 + 64] = 1.0
    d["wv"] = np.ascontiguousarray(Wv.reshape(16, 128, 520).transpose(1, 0, 2).reshape(128, 16 * 520))
    d["bv"] = bve
    p = np.arange(128)[:, None]
    cix = np.arange(16)[None, :]
    d["agi"] = np.ascontiguousarray(
        (((cix // 4) * 512 + (cix % 4) * 128 + p) * 4 + r).astype(np.int32)
    )
    return d


def kernel(
    L0_attn_x,
    L0_attn_Wqkv,
    L0_attn_b_qkv,
    L0_attn_Wout,
    L0_attn_b_out,
    L0_ffn_gate_kernel,
    L0_ffn_wi_0,
    L0_ffn_wi_1,
    L0_ffn_wo,
):
    x = np.asarray(L0_attn_x, dtype=np.float32)
    Wqkv = np.asarray(L0_attn_Wqkv, dtype=np.float32)
    b_qkv = np.asarray(L0_attn_b_qkv, dtype=np.float32)
    Wout = np.asarray(L0_attn_Wout, dtype=np.float32)
    b_out = np.asarray(L0_attn_b_out, dtype=np.float32)
    G = np.asarray(L0_ffn_gate_kernel, dtype=np.float32)
    wi0 = np.asarray(L0_ffn_wi_0, dtype=np.float32)
    wi1 = np.asarray(L0_ffn_wi_1, dtype=np.float32)
    wo = np.asarray(L0_ffn_wo, dtype=np.float32)

    shared = _stage_shared(Wqkv, b_qkv, Wout, b_out, G, wi0, wi1, wo)
    in_maps = []
    for c in range(8):
        m = dict(shared)
        m.update(_stage_core(c, x, Wqkv, b_qkv))
        in_maps.append(m)

    nc, _ = build(dbg=False)
    res = run_bass_kernel_spmd(nc, in_maps, core_ids=list(range(8)))
    out = np.empty((B, S, D), dtype=np.float32)
    for c in range(8):
        b, r = c // 4, c % 4
        out[b, r * TOKS : (r + 1) * TOKS] = res.results[c]["y"]
    return out


# version 17
# speedup vs baseline: 1.0697x; 1.0697x over previous
import contextlib
import numpy as np
import ml_dtypes
import concourse.bass as bass
import concourse.mybir as mybir
from concourse.bass_utils import run_bass_kernel_spmd

F32 = mybir.dt.float32
BF16 = mybir.dt.bfloat16
I32 = mybir.dt.int32
AF = mybir.ActivationFunctionType
ALU = mybir.AluOpType
AX = mybir.AxisListType
BF = ml_dtypes.bfloat16

B, S, D = 2, 4096, 2048
E = 8
TOKS = 1024
NCH, TC = 8, 512
EXPB = -20.0
ESC = 0.125
LN_EPS = 1e-5
INV_D = 1.0 / 2048.0
ENGINES = ("sync", "gpsimd", "vector", "scalar", "tensor")


class Em:
    def __init__(self, sems):
        self.sems = sems
        self.ops = {e: [] for e in ENGINES}
        self.cnt = {k: 0 for k in sems}

    def do(self, eng, fn, inc=None, by=1):
        if inc is None:
            self.ops[eng].append(fn)
            return None
        self.cnt[inc] += by
        tgt = self.cnt[inc]
        sem = self.sems[inc]

        def g(h, fn=fn, sem=sem, by=by):
            fn(h).then_inc(sem, by)

        self.ops[eng].append(g)
        return tgt

    def wait(self, eng, inc, val):
        if val is None or val <= 0:
            return
        sem = self.sems[inc]
        self.ops[eng].append(lambda h, sem=sem, val=val: h.wait_ge(sem, val))

    def barrier(self):
        self.cnt["bar"] += len(ENGINES)
        tgt = self.cnt["bar"]
        sem = self.sems["bar"]
        for e in ENGINES:
            self.ops[e].append(lambda h, sem=sem: h.sem_inc(sem, 1))
            self.ops[e].append(lambda h, sem=sem, tgt=tgt: h.wait_ge(sem, tgt))


def DMA(out, in_, **kw):
    return lambda h, out=out, in_=in_, kw=kw: h.dma_start(out=out, in_=in_, **kw)


def MM(out, l, r, st=True, sp=True):
    return lambda h, out=out, l=l, r=r, st=st, sp=sp: h.matmul(
        out, l, r, start=st, stop=sp
    )


def TT(out, a, b, op):
    return lambda h, out=out, a=a, b=b, op=op: h.tensor_tensor(out, a, b, op)


def CP(out, a):
    return lambda h, out=out, a=a: h.tensor_copy(out, a)


def build(dbg=False, upto=5):
    nc = bass.Bass("TRN2", target_bir_lowering=False, debug=False)

    t_xb = nc.dram_tensor("xb", [S, D], F32, kind="ExternalInput").ap()
    t_xown = nc.dram_tensor("xown", [TOKS, D], F32, kind="ExternalInput").ap()
    t_wqk = nc.dram_tensor("wqk", [128, 16 * 8 * 128], F32, kind="ExternalInput").ap()
    t_bqk = nc.dram_tensor("bqk", [128, 8], F32, kind="ExternalInput").ap()
    t_wv = nc.dram_tensor("wv", [128, 16 * 520], F32, kind="ExternalInput").ap()
    t_bv = nc.dram_tensor("bv", [1, 520], F32, kind="ExternalInput").ap()
    t_csc = nc.dram_tensor("csc", [128, S], F32, kind="ExternalInput").ap()
    t_css = nc.dram_tensor("css", [128, S], F32, kind="ExternalInput").ap()
    t_sel = nc.dram_tensor("sel", [128, 10 * 128], BF16, kind="ExternalInput").ap()
    t_msk = nc.dram_tensor("msk", [128, 4 * 512], BF16, kind="ExternalInput").ap()
    t_idn = nc.dram_tensor("idn", [128, 128], F32, kind="ExternalInput").ap()
    t_wout = nc.dram_tensor("wout", [4, 128, 16 * 512], F32, kind="ExternalInput").ap()
    t_bout = nc.dram_tensor("bout", [1, D], F32, kind="ExternalInput").ap()
    t_gate = nc.dram_tensor("gate", [128, 16 * 8], F32, kind="ExternalInput").ap()
    t_wi0 = nc.dram_tensor("wi0", [E, 4, 128, 16 * 512], F32, kind="ExternalInput").ap()
    t_wi1 = nc.dram_tensor("wi1", [E, 4, 128, 16 * 512], F32, kind="ExternalInput").ap()
    t_wo = nc.dram_tensor("wo", [E, 4, 128, 16 * 512], F32, kind="ExternalInput").ap()
    t_agi = nc.dram_tensor("agi", [128, 16], I32, kind="ExternalInput").ap()
    t_y = nc.dram_tensor("y", [TOKS, D], F32, kind="ExternalOutput").ap()

    cc_in_h = nc.dram_tensor("cc_in", [512, S], BF16)
    cc_out_h = nc.dram_tensor("cc_out", [4 * 512, S], BF16)

    dbg_t = {}
    if dbg:
        for nm, shape, dt in [
            ("d_lnxT", [128, 16 * 512], BF16),
            ("d_qT", [128, 4 * S], BF16),
            ("d_kT", [128, 4 * S], BF16),
            ("d_V", [128, 32 * 520], BF16),
            ("d_ctx", [128, 4 * S], BF16),
            ("d_ctxall", [128, 16 * 1024], BF16),
            ("d_h2T", [128, 16 * 1024], BF16),
            ("d_gate", [8, 1024], F32),
            ("d_cw", [128, 64], F32),
        ]:
            dbg_t[nm] = nc.dram_tensor(nm, shape, dt, kind="ExternalOutput").ap()

    xb_r = t_xb.rearrange("(n p) d -> n p d", p=128)
    xown_r = t_xown.rearrange("(n p) d -> n p d", p=128)
    y_r = t_y.rearrange("(n p) d -> n p d", p=128)
    y_q = t_y.rearrange("(n p) d -> p n d", p=128)

    es = contextlib.ExitStack()
    with es:
        sems = {
            n: es.enter_context(nc.semaphore(n))
            for n in ["lds", "ldp", "sts", "stp", "cc", "pe", "dve", "act", "bar",
                      "wv0", "wv1", "wv2", "wv3", "xa", "xb", "cs", "gat",
                      "woa", "wob", "wsa", "wsb", "wsc", "wq0", "wq1", "wq2", "wq3",
                      "wo0", "wo1", "wo2", "wo3", "wo4", "wo5", "wo6", "wo7",
                      "ya", "yb", "bn0", "bn1",
                      "p30", "p31", "p32", "p33", "p34", "p35", "p36", "p37"]
        }
        em = Em(sems)
        W = em.wait
        C = em.cnt

        A = nc.alloc_sbuf_tensor
        consts = A("consts", [128, 8], F32)
        ones_bf = A("ones_bf", [128, 128], BF16)
        ones_f = A("ones_f", [128, 128], F32)
        idn = A("idn_sb", [128, 128], F32)
        sel = A("sel_sb", [128, 10 * 128], BF16)
        msk = A("msk_sb", [128, 4 * 512], BF16)
        bqk = A("bqk_sb", [128, 8], F32)
        bv = A("bv_sb", [1, 520], BF16)
        bout = A("bout_sb", [1, D], BF16)
        gsb = A("gate_f32", [128, 128], F32)
        ghi = A("gate_hi", [128, 128], BF16)
        glo = A("gate_lo", [128, 128], BF16)
        cwsb = A("cw_sb", [128, 64], F32)
        agix = A("agi_sb", [128, 16], I32)
        wvt = A("wvt", [128, 4 * 520], BF16)

        B1 = A("B1", [128, 16384], BF16)
        B2 = A("B2", [128, 16384], BF16)
        B3 = A("B3", [128, 16384], BF16)
        B4 = A("B4", [128, 16640], BF16)
        B5 = A("B5", [128, 12288], BF16)

        F1 = A("F1", [128, 4096], F32)
        F2 = A("F2", [128, 2048], F32)
        F3 = A("F3", [128, 1024], F32)
        F4 = A("F4", [128, 1024], F32)
        F5 = A("F5", [128, 32], F32)

        PSA = nc.alloc_psum_tensor("PSA", [128, 1536], F32)
        PSB = nc.alloc_psum_tensor("PSB", [128, 1536], F32)
        PSC = nc.alloc_psum_tensor("PSC", [128, 512], F32)
        PSD = nc.alloc_psum_tensor("PSD", [128, 512], F32)

        CONSTS = [ESC, EXPB, INV_D, LN_EPS, -0.5, 1.0, 0.0, -INV_D]
        for i, v in enumerate(CONSTS):
            nc.const_aps.register(F32, v, consts[:, i : i + 1])

        block = es.enter_context(nc.Block())

        for i, v in enumerate(CONSTS):
            em.do("vector", lambda h, a=consts[:, i : i + 1], v=v: h.memset(a, v))
        em.do("vector", lambda h: h.memset(ones_bf[:], 1.0))
        em.do("vector", lambda h: h.memset(ones_f[:], 1.0), inc="dve")

        em.do("sync", DMA(idn[:], t_idn), inc="lds", by=16)
        em.do("sync", DMA(sel[:], t_sel), inc="lds", by=16)
        em.do("sync", DMA(msk[:], t_msk), inc="lds", by=16)
        em.do("sync", DMA(bqk[:], t_bqk), inc="lds", by=16)
        em.do("sync", DMA(agix[:], t_agi), inc="lds", by=16)
        em.do("sync", DMA(gsb[:], t_gate), inc="lds", by=16)
        ld_small = C["lds"]
        em.do("gpsimd", DMA(B1[:], t_wqk), inc="ldp", by=16)
        em.do("gpsimd", DMA(bv[:], t_bv), inc="ldp", by=16)
        em.do("gpsimd", DMA(bout[:], t_bout), inc="ldp", by=16)
        ld_wqk = C["ldp"]

        W("vector", "lds", ld_small)
        em.do("vector", CP(ghi[:], gsb[:]), inc="dve")
        W("vector", "dve", C["dve"])
        em.do("vector", TT(glo[:], gsb[:], ghi[:], ALU.subtract), inc="dve")

        em.barrier()

        def emit_ln(xs, junk):
            em.do("vector", lambda h, xs=xs: h.reduce_sum(F5[:, 0:1], xs, AX.X), inc="dve")
            W("vector", "dve", C["dve"])
            em.do(
                "vector",
                lambda h: h.tensor_scalar_mul(F5[:, 1:2], F5[:, 0:1], -INV_D),
                inc="dve",
            )
            W("scalar", "dve", C["dve"])
            em.do(
                "scalar",
                lambda h, xs=xs, junk=junk: h.activation(
                    junk, xs, AF.Square, bias=F5[:, 1:2], accum_out=F5[:, 2:3]
                ),
                inc="act",
            )
            W("scalar", "act", C["act"])
            em.do(
                "scalar",
                lambda h: h.activation(
                    F5[:, 3:4], F5[:, 2:3], AF.Ln, bias=LN_EPS, scale=INV_D
                ),
                inc="act",
            )
            W("scalar", "act", C["act"])
            em.do(
                "scalar",
                lambda h: h.activation(F5[:, 4:5], F5[:, 3:4], AF.Exp, scale=-0.5),
                inc="act",
            )
            W("vector", "act", C["act"])
            em.do(
                "vector",
                lambda h, xs=xs: h.tensor_scalar(
                    xs, xs, F5[:, 1:2], F5[:, 4:5], ALU.add, ALU.mult
                ),
                inc="dve",
            )

        W("tensor", "lds", ld_small)
        W("tensor", "ldp", ld_wqk)
        pe_lnxT_consumed = 0
        f1_guard = [0, 0]
        wvt_guard = [0, 0, 0, 0]
        tp_regs = [
            [PSA[:, 0:512], PSA[:, 512:1024], PSA[:, 1024:1536], PSC[:, 0:512]],
            [PSB[:, 0:512], PSB[:, 512:1024], PSB[:, 1024:1536], PSD[:, 0:512]],
        ]
        tp_region_guard = [[0] * 4, [0] * 4]
        B5v = B5[:, 0:8192].rearrange("p (dh f) -> p dh f", dh=16)
        dbg_done = {}

        for ch in range(NCH):
            W("sync", "dve", dbg_done.get("rope_prev", 0))
            em.do("sync", DMA(F4[:, 0:512], t_csc[:, ch * TC : (ch + 1) * TC]), inc="cs", by=16)
            em.do("sync", DMA(F4[:, 512:1024], t_css[:, ch * TC : (ch + 1) * TC]), inc="cs", by=16)
            cs_ld = C["cs"]
            wv_ld = [None] * 16
            for dh in range(4):
                W("gpsimd", "pe", wvt_guard[dh])
                em.do(
                    "gpsimd",
                    DMA(wvt[:, dh * 520 : (dh + 1) * 520], t_wv[:, dh * 520 : (dh + 1) * 520]),
                    inc=f"wv{dh}",
                    by=16,
                )
                wv_ld[dh] = (f"wv{dh}", C[f"wv{dh}"])

            for t4 in range(4):
                ti = ch * 4 + t4
                sl = ti % 2
                xs = F1[:, sl * 2048 : (sl + 1) * 2048]
                W("sync", "pe", f1_guard[sl])
                xsem = ["xa", "xb"][sl]
                em.do("sync", DMA(xs, xb_r[ti]), inc=xsem, by=16)
                W("vector", xsem, C[xsem])
                emit_ln(xs, F2[:, 0:2048])
                ln_done = C["dve"]
                W("tensor", "dve", ln_done)
                par4 = t4 % 2
                regs = tp_regs[par4]
                for g in range(4):
                    W("tensor", "dve", tp_region_guard[par4][g])
                    for q in range(4):
                        dh = g * 4 + q
                        ps = regs[g][:, q * 128 : (q + 1) * 128]
                        em.do(
                            "tensor",
                            lambda h, ps=ps, xs=xs, dh=dh: h.transpose(
                                ps, xs[:, dh * 128 : (dh + 1) * 128], idn[:]
                            ),
                            inc="pe" if q == 3 else None,
                        )
                    W("vector", "pe", C["pe"])
                    if t4 == 0 and g == 0:
                        W("vector", "pe", pe_lnxT_consumed)
                    dst = B5v[:, g * 4 : g * 4 + 4, t4 * 128 : t4 * 128 + 128]
                    src = regs[g].rearrange("p (a b) -> p a b", a=4)
                    tp_region_guard[par4][g] = em.do(
                        "vector", CP(dst, src), inc="dve"
                    )
                f1_guard[sl] = C["pe"]
            lnxT_ready = C["dve"]

            if dbg and ch == 0:
                W("sync", "dve", lnxT_ready)
                em.do("sync", DMA(dbg_t["d_lnxT"], B5[:, 0:8192]), inc="sts", by=16)

            W("tensor", "dve", lnxT_ready)
            evac_guard = [0, 0]
            qkv_pe = [0] * 8
            for ct in range(8):
                ps = [PSC, PSD][ct % 2]
                W("tensor", "dve", evac_guard[ct % 2])
                for dh in range(16):
                    em.do(
                        "tensor",
                        MM(
                            ps[:, 0:512],
                            B1[:, (dh * 8 + ct) * 128 : (dh * 8 + ct) * 128 + 128],
                            B5[:, dh * 512 : (dh + 1) * 512],
                            st=(dh == 0),
                            sp=(dh == 15),
                        ),
                        inc="pe" if dh == 15 else None,
                    )
                qkv_pe[ct] = C["pe"]
                W("vector", "pe", qkv_pe[ct])
                bcol = bqk[:, ct : ct + 1]
                if ct in (0, 1, 4, 5):
                    off = [0, 512, None, None, 1024, 1536][ct]
                    dst = F2[:, off : off + 512]
                    evac_guard[ct % 2] = em.do(
                        "vector",
                        lambda h, dst=dst, ps=ps, bcol=bcol: h.tensor_scalar(
                            dst, ps[:, 0:512], bcol, None, ALU.add
                        ),
                        inc="dve",
                    )
                else:
                    off = {2: 10240, 3: 10752, 6: 11264, 7: 11776}[ct]
                    dst = B5[:, off : off + 512]
                    evac_guard[ct % 2] = em.do(
                        "vector",
                        lambda h, dst=dst, ps=ps, bcol=bcol: h.tensor_scalar(
                            dst, ps[:, 0:512], bcol, None, ALU.add
                        ),
                        inc="dve",
                    )

            W("vector", "cs", cs_ld)
            CC, SS = F4[:, 0:512], F4[:, 512:1024]
            for base, rout in ((0, 8192), (1024, 9216)):
                u1 = F2[:, base : base + 512]
                u2 = F2[:, base + 512 : base + 1024]
                em.do("vector", TT(F3[:, 0:512], u1, CC, ALU.mult))
                em.do("vector", TT(F3[:, 512:1024], u2, SS, ALU.mult), inc="dve")
                W("vector", "dve", C["dve"])
                em.do(
                    "vector",
                    TT(B5[:, rout : rout + 512], F3[:, 0:512], F3[:, 512:1024], ALU.subtract),
                    inc="dve",
                )
                W("vector", "dve", C["dve"])
                em.do("vector", TT(F3[:, 0:512], u1, SS, ALU.mult))
                em.do("vector", TT(F3[:, 512:1024], u2, CC, ALU.mult), inc="dve")
                W("vector", "dve", C["dve"])
                em.do(
                    "vector",
                    TT(B5[:, rout + 512 : rout + 1024], F3[:, 0:512], F3[:, 512:1024], ALU.add),
                    inc="dve",
                )
            rope_done = C["dve"]
            dbg_done["rope_prev"] = rope_done

            W("tensor", "dve", rope_done)
            perm_evac = 0
            for isq, (rotoff, nroff, dstT) in enumerate(
                ((8192, 10240, B2), (9216, 11264, B3))
            ):
                for r in range(4):
                    W("tensor", "dve", perm_evac)
                    su1 = sel[:, r * 128 : (r + 1) * 128]
                    su2 = sel[:, (4 + r) * 128 : (5 + r) * 128]
                    snr = sel[:, (8 + r % 2) * 128 : (9 + r % 2) * 128]
                    nrsrc = B5[:, nroff + (r // 2) * 512 : nroff + (r // 2) * 512 + 512]
                    em.do("tensor", MM(PSA[:, 0:512], su1, B5[:, rotoff : rotoff + 512], st=True, sp=False))
                    em.do("tensor", MM(PSA[:, 0:512], su2, B5[:, rotoff + 512 : rotoff + 1024], st=False, sp=False))
                    em.do("tensor", MM(PSA[:, 0:512], snr, nrsrc, st=False, sp=True), inc="pe")
                    W("vector", "pe", C["pe"])
                    dst = dstT[:, r * S + ch * TC : r * S + ch * TC + 512]
                    perm_evac = em.do("vector", CP(dst, PSA[:, 0:512]), inc="dve")

            vmain = [PSA[:, 0:512], PSA[:, 512:1024], PSA[:, 1024:1536], PSB[:, 0:512]]
            vtail = [PSB[:, 512:520], PSB[:, 1024:1032], PSC[:, 128:136], PSD[:, 128:136]]
            W("tensor", "dve", perm_evac)
            for dh in range(16):
                slv = dh % 4
                W("tensor", wv_ld[dh][0], wv_ld[dh][1])
                for t4 in range(4):
                    lhs = B5[:, dh * 512 + t4 * 128 : dh * 512 + t4 * 128 + 128]
                    em.do(
                        "tensor",
                        MM(vmain[t4], lhs, wvt[:, slv * 520 : slv * 520 + 512], st=(dh == 0), sp=False),
                    )
                    em.do(
                        "tensor",
                        MM(vtail[t4], lhs, wvt[:, slv * 520 + 512 : slv * 520 + 520], st=(dh == 0), sp=False),
                        inc="pe" if t4 == 3 else None,
                    )
                wvt_guard[slv] = C["pe"]
                if dh + 4 < 16:
                    W("gpsimd", "pe", wvt_guard[slv])
                    em.do(
                        "gpsimd",
                        DMA(
                            wvt[:, slv * 520 : (slv + 1) * 520],
                            t_wv[:, (dh + 4) * 520 : (dh + 5) * 520],
                        ),
                        inc=f"wv{slv}",
                        by=16,
                    )
                    wv_ld[dh + 4] = (f"wv{slv}", C[f"wv{slv}"])
            for t4 in range(4):
                em.do("tensor", MM(vmain[t4], ones_bf[0:1, 0:128], bv[0:1, 0:512], st=False, sp=True))
                em.do(
                    "tensor",
                    MM(vtail[t4], ones_bf[0:1, 0:128], bv[0:1, 512:520], st=False, sp=True),
                    inc="pe",
                )
                W("vector", "pe", C["pe"])
                vc = (ch * 4 + t4) * 520
                em.do("vector", CP(B4[:, vc : vc + 512], vmain[t4]))
                em.do("vector", CP(B4[:, vc + 512 : vc + 520], vtail[t4]), inc="dve")
            pe_lnxT_consumed = C["pe"]
            tp_region_guard = [[C["dve"]] * 4, [C["dve"]] * 4]
            perm_evac = C["dve"]

        em.barrier()
        if dbg:
            em.do("sync", DMA(dbg_t["d_qT"], B2[:]), inc="sts", by=16)
            em.do("sync", DMA(dbg_t["d_kT"], B3[:]), inc="sts", by=16)
            em.do("sync", DMA(dbg_t["d_V"], B4[:, 0 : 32 * 520]), inc="sts", by=16)
            W("sync", "sts", C["sts"])
            em.barrier()

        if upto >= 1:
         sc_slots = [PSA[:, 0:1024], PSB[:, 0:1024]]
         bc_ps = [PSA[:, 1024:1536], PSB[:, 1024:1536]]
         av_ps = [PSC, PSD]
         slot_exp_act = [0, 0]
         attn_av_pe = [0, 0, 0]
         av_bank_dve = [C["dve"], C["dve"]]
         bc_free_dve = [C["dve"], C["dve"]]
         f3_free_dve = [C["dve"], C["dve"]]
         tmp_sts = [0, 0]
         ri = 0
         cc_q = cc_out_h.ap().rearrange("r (q t) -> (r q) t", q=4)
         for r in range(4):
             for j in range(8):
                 qof = j * TC
                 nkb = 4 * (j + 1)
                 G = nkb // 2
                 for s in range(2):
                     par = ri % 2
                     avb = av_ps[par]
                     hh = 2 * r + s
                     sc_pe = {}
                     exp_act = {}
                     msk_dve = {}

                     def emit_sc(g, s=s, r=r, qof=qof, sc_pe=sc_pe, exp_act=exp_act, msk_dve=msk_dve):
                         slot = sc_slots[g % 2]
                         W("tensor", "act", slot_exp_act[g % 2])
                         for i2 in range(2):
                             kb = 2 * g + i2
                             em.do(
                                 "tensor",
                                 MM(
                                     slot[:, i2 * 512 : i2 * 512 + 512],
                                     B3[64 * s : 64 * s + 64, r * S + kb * 128 : r * S + kb * 128 + 128],
                                     B2[64 * s : 64 * s + 64, r * S + qof : r * S + qof + 512],
                                 ),
                                 inc="pe" if i2 == 1 else None,
                             )
                         sc_pe[g] = C["pe"]
                         W("scalar", "pe", sc_pe[g])
                         W("scalar", "pe", attn_av_pe[g % 3])
                         aslot = B5[:, (g % 3) * 1024 : (g % 3) * 1024 + 1024]
                         em.do(
                             "scalar",
                             lambda h, aslot=aslot, slot=slot: h.activation(
                                 aslot, slot, AF.Exp, bias=EXPB, scale=ESC
                             ),
                             inc="act",
                         )
                         slot_exp_act[g % 2] = C["act"]
                         exp_act[g] = C["act"]
                         nd = 0
                         for i2 in range(2):
                             kb = 2 * g + i2
                             o = kb * 128 - qof
                             if o >= 0:
                                 W("vector", "act", exp_act[g])
                                 asl = B5[:, (g % 3) * 1024 + i2 * 512 : (g % 3) * 1024 + i2 * 512 + 512]
                                 oi = o // 128
                                 em.do(
                                     "vector",
                                     TT(asl, asl, msk[:, oi * 512 : oi * 512 + 512], ALU.mult),
                                     inc="dve",
                                 )
                                 nd += 1
                         msk_dve[g] = C["dve"] if nd else None

                     def emit_av(g, s=s, r=r, hh=hh, nkb=nkb, avb=avb, par=par, exp_act=exp_act, msk_dve=msk_dve):
                         W("tensor", "act", exp_act[g])
                         if msk_dve.get(g):
                             W("tensor", "dve", msk_dve[g])
                         if g == 0:
                             W("tensor", "dve", av_bank_dve[par])
                         for i2 in range(2):
                             kb = 2 * g + i2
                             em.do(
                                 "tensor",
                                 MM(
                                     avb[0:65, :],
                                     B4[:, kb * 520 + hh * 65 : kb * 520 + hh * 65 + 65],
                                     B5[:, (g % 3) * 1024 + i2 * 512 : (g % 3) * 1024 + i2 * 512 + 512],
                                     st=(kb == 0),
                                     sp=(kb == nkb - 1),
                                 ),
                                 inc="pe" if i2 == 1 else None,
                             )
                         attn_av_pe[g % 3] = C["pe"]

                     emit_sc(0)
                     if G > 1:
                         emit_sc(1)
                     for g in range(2, G):
                         emit_sc(g)
                         emit_av(g - 2)
                     if G > 1:
                         emit_av(G - 2)
                     emit_av(G - 1)
                     av_last = C["pe"]
                     W("vector", "pe", av_last)
                     frow = F4[64:65, par * 512 : par * 512 + 512]
                     em.do("vector", CP(frow, avb[64:65, :]), inc="dve")
                     W("tensor", "dve", C["dve"])
                     W("tensor", "dve", bc_free_dve[par])
                     em.do("tensor", MM(bc_ps[par], ones_f[64:65, 0:128], frow), inc="pe")
                     bc_pe = C["pe"]
                     W("vector", "pe", bc_pe)
                     W("vector", "dve", f3_free_dve[par])
                     f3s = F3[0:64, par * 512 : par * 512 + 512]
                     em.do(
                         "vector",
                         lambda h, f3s=f3s, b=bc_ps[par]: h.reciprocal(f3s, b[0:64, :]),
                         inc="dve",
                     )
                     bc_free_dve[par] = C["dve"]
                     if s == 0:
                         cdst = B1[0:64, r * S + qof : r * S + qof + 512]
                         em.do("vector", TT(cdst, avb[0:64, :], f3s, ALU.mult), inc="dve")
                     else:
                         W("vector", ["bn0", "bn1"][par], tmp_sts[par])
                         tmp = B5[0:64, 3072 + par * 512 : 3072 + par * 512 + 512]
                         em.do("vector", TT(tmp, avb[0:64, :], f3s, ALU.mult), inc="dve")
                         W("sync", "dve", C["dve"])
                         tmp_sts[par] = em.do(
                             "sync",
                             DMA(B1[64:128, r * S + qof : r * S + qof + 512], tmp),
                             inc=["bn0", "bn1"][par],
                             by=16,
                         )
                     av_bank_dve[par] = C["dve"]
                     f3_free_dve[par] = C["dve"]
                     ri += 1
             W("sync", "dve", C["dve"])
             W("sync", "bn0", C["bn0"])
             W("sync", "bn1", C["bn1"])
             em.do(
                 "sync",
                 DMA(cc_in_h.ap()[r * 128 : (r + 1) * 128, :], B1[:, r * S : (r + 1) * S]),
                 inc="sts",
                 by=16,
             )
             W("gpsimd", "sts", C["sts"])
             em.do(
                 "gpsimd",
                 lambda h, r=r: h.collective_compute(
                     "AllGather",
                     ALU.bypass,
                     replica_groups=[[0, 1, 2, 3], [4, 5, 6, 7]],
                     ins=[cc_in_h.ap()[r * 128 : (r + 1) * 128, :].opt()],
                     outs=[cc_out_h.ap()[r * 512 : (r + 1) * 512, :].opt()],
                 ),
                 inc="cc",
             )
             W("gpsimd", "cc", C["cc"])
             for rk in range(4):
                 c = r * 4 + rk
                 em.do(
                     "gpsimd",
                     lambda h, c=c: h.indirect_dma_start(
                         out=B1[:, c * 1024 : (c + 1) * 1024],
                         out_offset=None,
                         in_=cc_q[0:128, :],
                         in_offset=bass.IndirectOffsetOnAxis(ap=agix[:, c : c + 1], axis=0),
                     ),
                     inc="gat",
                     by=16,
                 )
             if r == 1:
                 W("gpsimd", "pe", C["pe"])
                 em.do("gpsimd", DMA(B3[:, 0:8192], t_wout[0]), inc="woa", by=16)
                 pre_wout = [("woa", C["woa"])]
             if r == 3:
                 W("gpsimd", "pe", C["pe"])
                 em.do("gpsimd", DMA(B3[:, 8192:16384], t_wout[1]), inc="wob", by=16)
                 pre_wout.append(("wob", C["wob"]))
         pre_x = []
         for tt in range(8):
             em.do("sync", DMA(F1[:, tt * 512 : tt * 512 + 512], xown_r[tt][:, 0:512]), inc=f"p3{tt}", by=16)
             pre_x.append((f"p3{tt}", C[f"p3{tt}"]))

         em.barrier()

        if upto >= 3:
         W("tensor", "gat", C["gat"])
         wout_guard = [C["pe"], C["pe"]]
         f1_g = [C["dve"]] * 8
         f2_g = [0, 0, 0, 0]
         ps4 = [PSA[:, 0:512], PSA[:, 512:1024], PSB[:, 0:512], PSB[:, 512:1024]]
         ps4_guard = [C["dve"]] * 4
         for dc in range(4):
             sl = dc % 2
             wsem = ["woa", "wob"][sl]
             if dc >= 2:
                 W("gpsimd", "pe", wout_guard[sl])
                 em.do(
                     "gpsimd",
                     DMA(B3[:, sl * 8192 : (sl + 1) * 8192], t_wout[dc]),
                     inc=wsem,
                     by=16,
                 )
                 wld = (wsem, C[wsem])
             else:
                 wld = pre_wout[dc]
             W("tensor", wld[0], wld[1])
             for tt in range(8):
                 it = dc * 8 + tt
                 fs = it % 8
                 if dc == 0:
                     xl = pre_x[tt]
                 else:
                     W("sync", "dve", f1_g[fs])
                     em.do(
                         "sync",
                         DMA(F1[:, fs * 512 : fs * 512 + 512], xown_r[tt][:, dc * 512 : dc * 512 + 512]),
                         inc=f"p3{fs}",
                         by=16,
                     )
                     xl = (f"p3{fs}", C[f"p3{fs}"])
                 ps = ps4[it % 4]
                 W("tensor", "dve", ps4_guard[it % 4])
                 for dh in range(16):
                     em.do(
                         "tensor",
                         MM(
                             ps,
                             B1[:, dh * 1024 + tt * 128 : dh * 1024 + tt * 128 + 128],
                             B3[:, sl * 8192 + dh * 512 : sl * 8192 + dh * 512 + 512],
                             st=(dh == 0),
                             sp=False,
                         ),
                     )
                 em.do(
                     "tensor",
                     MM(ps, ones_bf[0:1, 0:128], bout[0:1, dc * 512 : dc * 512 + 512], st=False, sp=True),
                     inc="pe",
                 )
                 mmd = C["pe"]
                 W("vector", "pe", mmd)
                 W("vector", xl[0], xl[1])
                 W("vector", "sts", f2_g[it % 4])
                 f2s = F2[:, (it % 4) * 512 : (it % 4) * 512 + 512]
                 em.do(
                     "vector",
                     TT(f2s, ps, F1[:, fs * 512 : fs * 512 + 512], ALU.add),
                     inc="dve",
                 )
                 ps4_guard[it % 4] = C["dve"]
                 f1_g[fs] = C["dve"]
                 W("sync", "dve", C["dve"])
                 f2_g[it % 4] = em.do(
                     "sync",
                     DMA(y_r[tt][:, dc * 512 : dc * 512 + 512], f2s),
                     inc="sts",
                     by=16,
                 )
             wout_guard[sl] = C["pe"]
             if dc == 2:
                 b3a_free_pe = C["pe"]
         em.do("gpsimd", DMA(B4[:, 0:8192], t_wi0[0, 0]), inc="wsc", by=16)
         pre_wi0_ld = ("wsc", C["wsc"])
         em.do("gpsimd", DMA(B4[:, 8192:16384], t_wo[0, 1]), inc="wo4", by=16)
         pre_wo1_ld = ("wo4", C["wo4"])
         W("gpsimd", "pe", b3a_free_pe)
         em.do("gpsimd", DMA(B3[:, 0:8192], t_wi1[0, 0]), inc="wq0", by=16)
         pre_wi1_ld = ("wq0", C["wq0"])
         W("sync", "sts", C["sts"])

         em.barrier()

        if upto >= 4:
         tpg = [C["dve"], C["dve"]]
         p4f1 = [C["pe"], C["pe"]]
         for tt in range(8):
             sl = tt % 2
             xs = F1[:, sl * 2048 : (sl + 1) * 2048]
             W("sync", "pe", p4f1[sl])
             ysem = ["ya", "yb"][sl]
             em.do("sync", DMA(xs, y_r[tt]), inc=ysem, by=16)
             W("vector", ysem, C[ysem])
             emit_ln(xs, F2[:, 0:2048])
             ln_done = C["dve"]
             W("tensor", "dve", ln_done)
             for dh in range(16):
                 ps = [PSC, PSD][dh % 2][:, 0:128]
                 W("tensor", "dve", tpg[dh % 2])
                 em.do(
                     "tensor",
                     lambda h, ps=ps, xs=xs, dh=dh: h.transpose(
                         ps, xs[:, dh * 128 : (dh + 1) * 128], idn[:]
                     ),
                     inc="pe",
                 )
                 W("scalar", "pe", C["pe"])
                 hidst = B1[:, dh * 1024 + tt * 128 : dh * 1024 + tt * 128 + 128]
                 em.do("scalar", lambda h, hidst=hidst, ps=ps: h.copy(hidst, ps), inc="act")
                 W("vector", "act", C["act"])
                 lodst = B5[:, dh * 512 + (tt % 4) * 128 : dh * 512 + (tt % 4) * 128 + 128]
                 tpg[dh % 2] = em.do(
                     "vector", TT(lodst, ps, hidst, ALU.subtract), inc="dve"
                 )
             p4f1[sl] = C["pe"]
             if tt % 4 == 3:
                 g4 = tt // 4
                 W("tensor", "dve", C["dve"])
                 for dh in range(16):
                     for term, (lw, rh) in enumerate(
                         (
                             (ghi, B1[:, dh * 1024 + g4 * 512 : dh * 1024 + g4 * 512 + 512]),
                             (glo, B1[:, dh * 1024 + g4 * 512 : dh * 1024 + g4 * 512 + 512]),
                             (ghi, B5[:, dh * 512 : dh * 512 + 512]),
                         )
                     ):
                         em.do(
                             "tensor",
                             MM(
                                 PSC[0:8, :],
                                 lw[:, dh * 8 : dh * 8 + 8],
                                 rh,
                                 st=(dh == 0 and term == 0),
                                 sp=(dh == 15 and term == 2),
                             ),
                             inc="pe" if (dh == 15 and term == 2) else None,
                         )
                 W("vector", "pe", C["pe"])
                 em.do(
                     "vector",
                     CP(F3[0:8, g4 * 512 : g4 * 512 + 512], PSC[0:8, :]),
                     inc="dve",
                 )
                 h2lo_guard = C["dve"]
                 gate_pe = C["pe"]
         W("gpsimd", "pe", gate_pe)
         em.do("gpsimd", DMA(B5[:, 0:8192], t_wo[0, 0]), inc="wo0", by=16)
         pre_wo0_ld = ("wo0", C["wo0"])
         if dbg:
             W("sync", "dve", C["dve"])
             em.do("sync", DMA(dbg_t["d_h2T"], B1[:]), inc="sts", by=16)
             em.do("sync", DMA(dbg_t["d_gate"], F3[0:8, :]), inc="sts", by=16)

         for tt in range(8):
             W("tensor", "dve", C["dve"])
             em.do(
                 "tensor",
                 lambda h, tt=tt: h.transpose(
                     PSD[0:128, 0:8],
                     F3[0:8, tt * 128 : tt * 128 + 128],
                     idn[0:8, 0:8],
                 ),
                 inc="pe",
             )
             W("vector", "pe", C["pe"])
             em.do("vector", CP(F5[:, 0:8], PSD[:, 0:8]), inc="dve")
             W("vector", "dve", C["dve"])
             em.do("vector", lambda h: h.reduce_max(F5[:, 8:9], F5[:, 0:8], AX.X), inc="dve")
             W("vector", "dve", C["dve"])
             em.do(
                 "vector",
                 lambda h: h.tensor_scalar(F5[:, 16:24], F5[:, 0:8], F5[:, 8:9], None, ALU.is_equal),
                 inc="dve",
             )
             W("vector", "dve", C["dve"])
             em.do(
                 "vector",
                 lambda h: h.scalar_tensor_tensor(
                     F5[:, 24:32], F5[:, 16:24], -1e30, F5[:, 0:8], ALU.mult, ALU.add
                 ),
                 inc="dve",
             )
             W("vector", "dve", C["dve"])
             em.do("vector", lambda h: h.reduce_max(F5[:, 9:10], F5[:, 24:32], AX.X), inc="dve")
             W("vector", "dve", C["dve"])
             em.do(
                 "vector",
                 lambda h: h.tensor_scalar(F5[:, 10:11], F5[:, 9:10], F5[:, 8:9], None, ALU.subtract),
                 inc="dve",
             )
             W("scalar", "dve", C["dve"])
             em.do("scalar", lambda h: h.activation(F5[:, 11:12], F5[:, 10:11], AF.Exp), inc="act")
             W("vector", "act", C["act"])
             em.do("vector", lambda h: h.tensor_scalar_add(F5[:, 12:13], F5[:, 11:12], 1.0), inc="dve")
             W("vector", "dve", C["dve"])
             em.do("vector", lambda h: h.reciprocal(F5[:, 13:14], F5[:, 12:13]), inc="dve")
             W("vector", "dve", C["dve"])
             em.do("vector", TT(F5[:, 14:15], F5[:, 11:12], F5[:, 13:14], ALU.mult), inc="dve")
             W("vector", "dve", C["dve"])
             em.do(
                 "vector",
                 lambda h: h.tensor_scalar(F5[:, 16:24], F5[:, 16:24], F5[:, 13:14], None, ALU.mult),
             )
             em.do(
                 "vector",
                 lambda h: h.tensor_scalar(F5[:, 24:32], F5[:, 24:32], F5[:, 9:10], None, ALU.is_equal),
                 inc="dve",
             )
             W("vector", "dve", C["dve"])
             em.do(
                 "vector",
                 lambda h, tt=tt: h.scalar_tensor_tensor(
                     cwsb[:, tt * 8 : tt * 8 + 8],
                     F5[:, 24:32],
                     F5[:, 14:15],
                     F5[:, 16:24],
                     ALU.mult,
                     ALU.add,
                 ),
                 inc="dve",
             )
         if dbg:
             W("sync", "dve", C["dve"])
             em.do("sync", DMA(dbg_t["d_cw"], cwsb[:]), inc="sts", by=16)

         em.barrier()

        if upto >= 5:
         wi_slots = [
             B3[:, 0:8192],
             B3[:, 8192:16384],
             B4[:, 0:8192],
         ]
         wi_guard = [C["pe"]] * 3
         wo_blks = [B5[:, 0:8192], B4[:, 8192:16384]]
         wo_q_ld = {0: [pre_wo0_ld] * 4, 1: [pre_wo1_ld] * 4}
         for _n in range(2, 4 * E):
             wo_q_ld[_n] = [None] * 4
         half_evac = [C["dve"], C["dve"]]
         seq = 2
         hps = [
             (PSA[:, 0:512], PSA[:, 512:1024]),
             (PSB[:, 0:512], PSB[:, 512:1024]),
         ]
         hps_guard = [C["dve"], C["dve"]]
         sil_guard = [C["dve"], C["dve"]]
         wops8 = [
             PSA[:, 0:512], PSA[:, 512:1024], PSA[:, 1024:1536],
             PSB[:, 0:512], PSB[:, 512:1024], PSB[:, 1024:1536],
             PSC[:, 0:512], PSD[:, 0:512],
         ]
         slot6 = [
             F2[:, 0:512], F2[:, 512:1024], F2[:, 1024:1536], F2[:, 1536:2048],
             F4[:, 0:512], F4[:, 512:1024],
         ]
         slot6_guard = [0] * 6
         y_wr_cnt = [0, 0, 0, 0]
         y_wr_new = [0, 0, 0, 0]
         wo_evac_dve = C["dve"]
         l1q = [pre_wi1_ld] * 4
         l0 = pre_wi0_ld
         wi_sems = ["wsa", "wsb", "wsc"]
         for e in range(E):
             for mc in range(4):
                 m = e * 4 + mc
                 s0 = seq % 3
                 s1 = (seq + 1) % 3
                 third = (seq + 2) % 3
                 seq += 2
                 if m + 1 < 4 * E:
                     en, mn = (m + 1) // 4, (m + 1) % 4
                     W("gpsimd", "pe", wi_guard[third])
                     em.do("gpsimd", DMA(wi_slots[third], t_wi0[en, mn]), inc=wi_sems[third], by=16)
                     l0_next = (wi_sems[third], C[wi_sems[third]])
                 l1q_next = [None] * 4
                 for ms in range(4):
                     for th in range(2):
                         it = ms * 2 + th
                         p01 = hps[it % 2]
                         W("tensor", "dve", hps_guard[it % 2])
                         if ms == 0 and th == 0:
                             W("tensor", l0[0], l0[1])
                         for dh in range(16):
                             em.do(
                                 "tensor",
                                 MM(
                                     p01[0],
                                     wi_slots[s0][:, ms * 2048 + dh * 128 : ms * 2048 + dh * 128 + 128],
                                     B1[:, dh * 1024 + th * 512 : dh * 1024 + th * 512 + 512],
                                     st=(dh == 0),
                                     sp=(dh == 15),
                                 ),
                                 inc="pe" if dh == 15 else None,
                             )
                         h0_pe = C["pe"]
                         if th == 0:
                             W("tensor", l1q[ms][0], l1q[ms][1])
                         for dh in range(16):
                             em.do(
                                 "tensor",
                                 MM(
                                     p01[1],
                                     wi_slots[s1][:, ms * 2048 + dh * 128 : ms * 2048 + dh * 128 + 128],
                                     B1[:, dh * 1024 + th * 512 : dh * 1024 + th * 512 + 512],
                                     st=(dh == 0),
                                     sp=(dh == 15),
                                 ),
                                 inc="pe" if dh == 15 else None,
                             )
                         h1_pe = C["pe"]
                         W("scalar", "pe", h0_pe)
                         W("scalar", "dve", sil_guard[it % 2])
                         fsl = F3[:, (it % 2) * 512 : (it % 2) * 512 + 512]
                         em.do(
                             "scalar",
                             lambda h, fsl=fsl, p=p01[0]: h.activation(fsl, p, AF.Silu),
                             inc="act",
                         )
                         sact = C["act"]
                         W("vector", "act", sact)
                         W("vector", "pe", h1_pe)
                         hdst = B2[:, (mc * 4 + ms) * 1024 + th * 512 : (mc * 4 + ms) * 1024 + th * 512 + 512]
                         em.do("vector", TT(hdst, fsl, p01[1], ALU.mult), inc="dve")
                         sil_guard[it % 2] = C["dve"]
                         hps_guard[it % 2] = C["dve"]
                     if m + 1 < 4 * E:
                         W("gpsimd", "pe", C["pe"])
                         em.do(
                             "gpsimd",
                             DMA(
                                 wi_slots[s0][:, ms * 2048 : (ms + 1) * 2048],
                                 t_wi1[en, mn][:, ms * 2048 : (ms + 1) * 2048],
                             ),
                             inc=f"wq{ms}",
                             by=16,
                         )
                         l1q_next[ms] = (f"wq{ms}", C[f"wq{ms}"])
                 wi_guard[s0] = C["pe"]
                 wi_guard[s1] = C["pe"]
                 if m + 1 < 4 * E:
                     l0, l1q = l0_next, l1q_next
             h_done_dve = C["dve"]

             W("sync", "dve", wo_evac_dve)
             W("sync", "stp", y_wr_cnt[0])
             em.do(
                 "sync",
                 DMA(
                     F1[:, 0:4096].rearrange("p (t c) -> p t c", t=8),
                     y_q[:, :, 0:512],
                 ),
                 inc="lds",
                 by=16,
             )
             y_rd = C["lds"]
             for dc in range(4):
                 n = e * 4 + dc
                 blk = wo_blks[n % 2]
                 if dc == 0:
                     W("tensor", "dve", h_done_dve)
                 for half in range(2):
                     W("tensor", "dve", half_evac[half])
                     for mh in range(16):
                         if mh % 4 == 0:
                             W("tensor", wo_q_ld[n][mh // 4][0], wo_q_ld[n][mh // 4][1])
                         for t4 in range(4):
                             tt = half * 4 + t4
                             em.do(
                                 "tensor",
                                 MM(
                                     wops8[tt],
                                     B2[:, mh * 1024 + tt * 128 : mh * 1024 + tt * 128 + 128],
                                     blk[:, mh * 512 : mh * 512 + 512],
                                     st=(mh == 0),
                                     sp=(mh == 15),
                                 ),
                                 inc="pe" if t4 == 3 else None,
                             )
                         if half == 1 and mh % 4 == 3 and n + 2 < 4 * E:
                             q = mh // 4
                             nb = wo_blks[(n + 2) % 2]
                             wsem2 = f"wo{((n + 2) % 2) * 4 + q}"
                             W("gpsimd", "pe", C["pe"])
                             em.do(
                                 "gpsimd",
                                 DMA(
                                     nb[:, q * 2048 : (q + 1) * 2048],
                                     t_wo[(n + 2) // 4, (n + 2) % 4][:, q * 2048 : (q + 1) * 2048],
                                 ),
                                 inc=wsem2,
                                 by=16,
                             )
                             wo_q_ld[n + 2][q] = (wsem2, C[wsem2])
                     half_pe = C["pe"]
                     W("vector", "pe", half_pe)
                     if half == 0:
                         W("vector", "lds", y_rd)
                     for t4 in range(4):
                         tt = half * 4 + t4
                         g6 = (dc * 8 + tt) % 6
                         W("vector", "stp", slot6_guard[g6])
                         sl6 = slot6[g6]
                         cwc = cwsb[:, tt * 8 + e : tt * 8 + e + 1]
                         em.do(
                             "vector",
                             lambda h, sl6=sl6, ps=wops8[tt], cwc=cwc, tt=tt: h.scalar_tensor_tensor(
                                 sl6, ps, cwc, F1[:, tt * 512 : tt * 512 + 512], ALU.mult, ALU.add
                             ),
                             inc="dve",
                         )
                         W("scalar", "dve", C["dve"])
                         slot6_guard[g6] = em.do(
                             "scalar",
                             DMA(y_r[tt][:, dc * 512 : dc * 512 + 512], sl6),
                             inc="stp",
                             by=16,
                         )
                     half_evac[half] = C["dve"]
                 wo_evac_dve = C["dve"]
                 y_wr_new[dc] = C["stp"]
                 if dc < 3:
                     W("sync", "dve", wo_evac_dve)
                     W("sync", "stp", y_wr_cnt[dc + 1])
                     em.do(
                         "sync",
                         DMA(
                             F1[:, 0:4096].rearrange("p (t c) -> p t c", t=8),
                             y_q[:, :, (dc + 1) * 512 : (dc + 2) * 512],
                         ),
                         inc="lds",
                         by=16,
                     )
                     y_rd = C["lds"]
             y_wr_cnt = list(y_wr_new)
             hps_guard = [wo_evac_dve, wo_evac_dve]

        W("gpsimd", "stp", C["stp"])
        W("sync", "sts", C["sts"])
        em.barrier()

        @block.sync
        def _(h):
            for f in em.ops["sync"]:
                f(h)

        @block.gpsimd
        def _(h):
            for f in em.ops["gpsimd"]:
                f(h)

        @block.vector
        def _(h):
            for f in em.ops["vector"]:
                f(h)

        @block.scalar
        def _(h):
            for f in em.ops["scalar"]:
                f(h)

        @block.tensor
        def _(h):
            for f in em.ops["tensor"]:
                f(h)

    return nc, dbg_t


def _stage_shared(Wqkv, b_qkv, Wout, b_out, G, wi0, wi1, wo):
    st = {}
    st["csc"], st["css"] = _rope_tables()
    st["sel"] = _selectors()
    st["msk"] = _masks()
    st["idn"] = np.eye(128, dtype=np.float32)
    cidx = np.arange(16)[:, None]
    perm = ((cidx % 4) * 512 + (cidx // 4) * 128 + np.arange(128)[None, :]).reshape(-1)
    st["wout"] = np.ascontiguousarray(
        Wout[perm].reshape(16, 128, 4, 512).transpose(2, 1, 0, 3).reshape(4, 128, 16 * 512)
    )
    st["bout"] = np.ascontiguousarray(b_out.reshape(1, D))
    st["gate"] = np.ascontiguousarray(G.reshape(16, 128, 8).transpose(1, 0, 2).reshape(128, 128))
    tr = lambda w: np.ascontiguousarray(
        w.reshape(E, 16, 128, 4, 512).transpose(0, 3, 2, 1, 4).reshape(E, 4, 128, 16 * 512)
    )
    tr_wi = lambda w: np.ascontiguousarray(
        w.reshape(E, 16, 128, 4, 4, 128).transpose(0, 3, 2, 4, 1, 5).reshape(E, 4, 128, 16 * 512)
    )
    st["wi0"], st["wi1"], st["wo"] = tr_wi(wi0), tr_wi(wi1), tr(wo)
    return st


def _rope_tables():
    invf = (10000.0 ** (-(np.arange(0, 32, 2, dtype=np.float64)) / 32.0)).astype(np.float64)
    t = np.arange(S, dtype=np.float64)
    f = np.outer(t, invf)
    c = np.cos(f).astype(np.float32).T
    s = np.sin(f).astype(np.float32).T
    return (
        np.ascontiguousarray(np.tile(c, (8, 1))),
        np.ascontiguousarray(np.tile(s, (8, 1))),
    )


def _selectors():
    sel = np.zeros((128, 10 * 128), dtype=BF)
    for r in range(4):
        u1 = np.zeros((128, 128), dtype=np.float32)
        u2 = np.zeros((128, 128), dtype=np.float32)
        for m in range(16):
            u1[32 * r + m, m] = 1.0
            u1[32 * r + 16 + m, 64 + m] = 1.0
            u2[32 * r + m, 16 + m] = 1.0
            u2[32 * r + 16 + m, 80 + m] = 1.0
        sel[:, r * 128 : (r + 1) * 128] = u1.astype(BF)
        sel[:, (4 + r) * 128 : (5 + r) * 128] = u2.astype(BF)
    nrA = np.zeros((128, 128), dtype=np.float32)
    nrB = np.zeros((128, 128), dtype=np.float32)
    for m in range(32):
        nrA[m, 32 + m] = 1.0
        nrA[32 + m, 96 + m] = 1.0
        nrB[64 + m, 32 + m] = 1.0
        nrB[96 + m, 96 + m] = 1.0
    sel[:, 8 * 128 : 9 * 128] = nrA.astype(BF)
    sel[:, 9 * 128 : 10 * 128] = nrB.astype(BF)
    return sel


def _masks():
    m = np.zeros((128, 4 * 512), dtype=BF)
    ki = np.arange(128)[:, None]
    qi = np.arange(512)[None, :]
    for oi, o in enumerate((0, 128, 256, 384)):
        m[:, oi * 512 : (oi + 1) * 512] = (qi >= ki + o).astype(BF)
    return m


def _stage_core(c, x, Wqkv, b_qkv):
    b, r = c // 4, c % 4
    heads = [8 * r + i for i in range(8)]
    d = {}
    d["xb"] = np.ascontiguousarray(x[b])
    d["xown"] = np.ascontiguousarray(x[b, r * TOKS : (r + 1) * TOKS])
    qu1 = [h * 64 + f for h in heads for f in range(16)]
    qu2 = [h * 64 + 16 + f for h in heads for f in range(16)]
    qnr = [h * 64 + 32 + dd for h in heads for dd in range(32)]
    cols = (
        qu1
        + qu2
        + qnr[:128]
        + qnr[128:]
        + [2048 + cc for cc in qu1]
        + [2048 + cc for cc in qu2]
        + [2048 + cc for cc in qnr[:128]]
        + [2048 + cc for cc in qnr[128:]]
    )
    Wqk = Wqkv[:, cols]
    d["wqk"] = np.ascontiguousarray(
        Wqk.reshape(16, 128, 8, 128).transpose(1, 0, 2, 3).reshape(128, 16 * 8 * 128)
    )
    d["bqk"] = np.ascontiguousarray(b_qkv[cols].reshape(8, 128).T)
    Wv = np.zeros((D, 520), dtype=np.float32)
    bve = np.zeros((1, 520), dtype=np.float32)
    for hh, h in enumerate(heads):
        Wv[:, hh * 65 : hh * 65 + 64] = Wqkv[:, 4096 + h * 64 : 4096 + h * 64 + 64]
        bve[0, hh * 65 : hh * 65 + 64] = b_qkv[4096 + h * 64 : 4096 + h * 64 + 64]
        bve[0, hh * 65 + 64] = 1.0
    d["wv"] = np.ascontiguousarray(Wv.reshape(16, 128, 520).transpose(1, 0, 2).reshape(128, 16 * 520))
    d["bv"] = bve
    p = np.arange(128)[:, None]
    cix = np.arange(16)[None, :]
    d["agi"] = np.ascontiguousarray(
        (((cix // 4) * 512 + (cix % 4) * 128 + p) * 4 + r).astype(np.int32)
    )
    return d


def kernel(
    L0_attn_x,
    L0_attn_Wqkv,
    L0_attn_b_qkv,
    L0_attn_Wout,
    L0_attn_b_out,
    L0_ffn_gate_kernel,
    L0_ffn_wi_0,
    L0_ffn_wi_1,
    L0_ffn_wo,
):
    x = np.asarray(L0_attn_x, dtype=np.float32)
    Wqkv = np.asarray(L0_attn_Wqkv, dtype=np.float32)
    b_qkv = np.asarray(L0_attn_b_qkv, dtype=np.float32)
    Wout = np.asarray(L0_attn_Wout, dtype=np.float32)
    b_out = np.asarray(L0_attn_b_out, dtype=np.float32)
    G = np.asarray(L0_ffn_gate_kernel, dtype=np.float32)
    wi0 = np.asarray(L0_ffn_wi_0, dtype=np.float32)
    wi1 = np.asarray(L0_ffn_wi_1, dtype=np.float32)
    wo = np.asarray(L0_ffn_wo, dtype=np.float32)

    shared = _stage_shared(Wqkv, b_qkv, Wout, b_out, G, wi0, wi1, wo)
    in_maps = []
    for c in range(8):
        m = dict(shared)
        m.update(_stage_core(c, x, Wqkv, b_qkv))
        in_maps.append(m)

    nc, _ = build(dbg=False)
    res = run_bass_kernel_spmd(nc, in_maps, core_ids=list(range(8)))
    out = np.empty((B, S, D), dtype=np.float32)
    for c in range(8):
        b, r = c // 4, c % 4
        out[b, r * TOKS : (r + 1) * TOKS] = res.results[c]["y"]
    return out
